# version 08
# baseline (speedup 1.0000x reference)
import numpy as np
from contextlib import ExitStack

import concourse.bass as bass
import concourse.mybir as mybir
from concourse.bass_utils import run_bass_kernel_spmd

F32 = mybir.dt.float32
BF16 = mybir.dt.bfloat16
AF = mybir.ActivationFunctionType
ALU = mybir.AluOpType
AX = mybir.AxisListType

ENGS = ("te", "act", "dve", "gp", "sp")


def full_cfg():
    return dict(B=2, S=8192, D=1024, H=16, FF=4096, E=8, NCORES=8)


class Sem:
    __slots__ = ("h", "n", "name")

    def __init__(self, h, name):
        self.h = h
        self.n = 0
        self.name = name

    def bump(self, d):
        self.n += d
        return self.n


class Q:
    def __init__(self):
        self.items = []

    def wait(self, sem, val):
        if val and val > 0:
            self.items.append(("w", sem, val))

    def op(self, fn, inc=None, d=1):
        tgt = inc.bump(d) if inc is not None else 0
        self.items.append(("o", fn, inc, d))
        return tgt


def build(cfg):
    B, S, D, H, FF, E, NCORES = (cfg[k] for k in ("B", "S", "D", "H", "FF", "E", "NCORES"))
    DH = 64
    SCALE = DH ** -0.5
    CPB = NCORES // B
    T = S // CPB
    DC = D // 128
    NP = H // 2
    NW = S // 512
    NT1 = S // 128
    TT = T // 128
    NG = T // 512
    KC = S // 128
    NSB = KC // 2
    FQ = FF // 1024
    NH2 = D // 512
    EPS = 1e-5

    nc = bass.Bass("TRN2", target_bir_lowering=False)

    x_in = nc.dram_tensor("x", [S, D], F32, kind="ExternalInput").ap()
    wqkv_in = nc.dram_tensor("wqkv", [D, 3 * D], F32, kind="ExternalInput").ap()
    bqkv_in = nc.dram_tensor("bqkv", [3 * D], F32, kind="ExternalInput").ap()
    wproj_in = nc.dram_tensor("wproj", [D, D], F32, kind="ExternalInput").ap()
    bproj_in = nc.dram_tensor("bproj", [D], F32, kind="ExternalInput").ap()
    wg_in = nc.dram_tensor("wg", [E, D, FF], F32, kind="ExternalInput").ap()
    wu_in = nc.dram_tensor("wu", [E, D, FF], F32, kind="ExternalInput").ap()
    wd_in = nc.dram_tensor("wd", [E, FF, D], F32, kind="ExternalInput").ap()
    out_hbm = nc.dram_tensor("out", [T, D], F32, kind="ExternalOutput").ap()
    xln1_hbm = nc.dram_tensor("xln1", [D, S], BF16, kind="Internal").ap()
    xln1_r = xln1_hbm.rearrange("(c p) t -> p c t", p=128)
    DBG = cfg.get("debug", False)
    if DBG:
        d_xln1 = nc.dram_tensor("d_xln1", [D, S], BF16, kind="ExternalOutput").ap()
        d_attn = nc.dram_tensor("d_attn", [128, NP * T], BF16, kind="ExternalOutput").ap()
        d_x1 = nc.dram_tensor("d_x1", [T, D], F32, kind="ExternalOutput").ap()
        d_xln2 = nc.dram_tensor("d_xln2", [128, DC * T], BF16, kind="ExternalOutput").ap()

    qs = {e: Q() for e in ENGS}
    te, act, dve, gp, sp = (qs[e] for e in ENGS)

    with ExitStack() as root:
        sems = {}
        for name in ("sSP", "sGP", "sGPc", "sTE", "sAC", "sDV"):
            sems[name] = Sem(root.enter_context(nc.semaphore(name)), name)
        sSP, sGP, sGPc, sTE, sAC, sDV = (sems[n] for n in
                                         ("sSP", "sGP", "sGPc", "sTE", "sAC", "sDV"))
        all_sems = list(sems.values())

        def barrier():
            snap = [(s, s.n) for s in all_sems]
            for e in ENGS:
                for s, v in snap:
                    qs[e].wait(s, v)

        ident_b = root.enter_context(nc.sbuf_tensor("ident_b", [128, 128], BF16))
        ident_f = root.enter_context(nc.sbuf_tensor("ident_f", [128, 128], F32))
        ones_b = root.enter_context(nc.sbuf_tensor("ones_b", [1, 512], BF16))
        ones_col = root.enter_context(nc.sbuf_tensor("ones_col", [128, 1], BF16))
        bq_row = root.enter_context(nc.sbuf_tensor("bq_row", [1, 3 * D], BF16))
        bp_row = root.enter_context(nc.sbuf_tensor("bp_row", [1, D], BF16))
        eps_col = root.enter_context(nc.sbuf_tensor("eps_col", [128, 1], F32))

        gp.op(lambda g: g.memset(ident_b[:, :], 0.0))
        gp.op(lambda g: g.affine_select(
            out=ident_b[:, :], in_=ident_b[:, :], compare_op=ALU.not_equal,
            fill=1.0, base=0, pattern=[[-1, 128]], channel_multiplier=1))
        gp.op(lambda g: g.memset(ident_f[:, :], 0.0))
        gp.op(lambda g: g.affine_select(
            out=ident_f[:, :], in_=ident_f[:, :], compare_op=ALU.not_equal,
            fill=1.0, base=0, pattern=[[-1, 128]], channel_multiplier=1))
        gp.op(lambda g: g.memset(ones_b[:, :], 1.0))
        gp.op(lambda g: g.memset(eps_col[:, :], EPS))
        v_consts = gp.op(lambda g: g.memset(ones_col[:, :], 1.0), inc=sGPc)
        gp.op(lambda g: g.dma_start(bq_row[0:1, :], bqkv_in[None, :]), inc=sGP, d=16)
        v_brows = gp.op(lambda g: g.dma_start(bp_row[0:1, :], bproj_in[None, :]),
                        inc=sGP, d=16)

        def emit_ln(x_ap, out_ap, stp, sq_ap, w_first=(), prev=None):
            ssum, negmean, msq, svar, slnv, srstd = (
                stp[:, i:i + 1] for i in range(6))
            sss = stp[:, 7:8]
            p = prev or dict(a1=0, a2=0, d1=0, d2=0, g0=0)
            for s, v in w_first:
                act.wait(s, v)
            act.wait(sDV, p["d2"])
            act.wait(sGPc, p["g0"])
            a0 = act.op(lambda a: a.activation(sq_ap, x_ap, AF.Square,
                                               accum_out=sss), inc=sAC)
            for s, v in w_first:
                dve.wait(s, v)
            dve.wait(sAC, p["a1"])
            d0 = dve.op(lambda v: v.reduce_sum(ssum, x_ap, axis=AX.X), inc=sDV)
            act.wait(sDV, d0)
            a1 = act.op(lambda a: a.activation(negmean, ssum, AF.Copy,
                                               scale=-1.0 / D), inc=sAC)
            gp.wait(sAC, a1)
            gp.wait(sDV, p["d1"])
            g0 = gp.op(lambda g: g.tensor_mul(msq, negmean, negmean), inc=sGPc)
            dve.wait(sAC, a0)
            dve.wait(sGPc, g0)
            d1 = dve.op(lambda v: v.scalar_tensor_tensor(
                svar, sss, 1.0 / D, msq, ALU.mult, ALU.subtract), inc=sDV)
            act.wait(sDV, d1)
            aL = act.op(lambda a: a.activation(slnv, svar, AF.Ln,
                                               bias=eps_col[:, 0:1]), inc=sAC)
            act.wait(sAC, aL)
            a2 = act.op(lambda a: a.activation(srstd, slnv, AF.Exp, scale=-0.5),
                        inc=sAC)
            dve.wait(sAC, a2)
            d2 = dve.op(lambda v: v.tensor_scalar(
                out_ap, x_ap, negmean, srstd, ALU.add, ALU.mult), inc=sDV)
            return dict(a1=a1, a2=a2, d1=d1, d2=d2, g0=g0)

        with ExitStack() as ph1:
            xt = ph1.enter_context(nc.sbuf_tensor("p1_xt", [128, 4, D], F32))
            sq = ph1.enter_context(nc.sbuf_tensor("p1_sq", [128, D], F32))
            xl = ph1.enter_context(nc.sbuf_tensor("p1_xl", [128, 4, D], BF16))
            win = ph1.enter_context(nc.sbuf_tensor("p1_win", [128, 2, DC, 512], BF16))
            st1 = ph1.enter_context(nc.sbuf_tensor("p1_st", [128, 4, 8], F32))
            tpp = ph1.enter_context(nc.psum_tensor("p1_tp", [128, 2, DC, 128], BF16))

            xt_free = [0, 0, 0, 0]
            xl_free = [0, 0, 0, 0]
            tpp_free = [0, 0]
            win_free = [0, 0]
            wrote = [0] * NW
            act.wait(sGPc, v_consts)
            dve.wait(sGPc, v_consts)
            te.wait(sGPc, v_consts)

            ln_prev = [None, None, None, None]
            for i in range(NT1):
                par = i % 4
                w, sub = i // 4, i % 4
                sp.wait(sDV, xt_free[par])
                ld = sp.op(lambda s, i=i, par=par: s.dma_start(
                    xt[:, par, :], x_in[i * 128:(i + 1) * 128, :]), inc=sSP, d=16)
                dve.wait(sTE, xl_free[par])
                ln_prev[par] = emit_ln(xt[:, par, :], xl[:, par, :],
                                       st1[:, par, :], sq[:, :],
                                       w_first=[(sSP, ld)], prev=ln_prev[par])
                d_ln = ln_prev[par]["d2"]
                xt_free[par] = d_ln
                te.wait(sDV, d_ln)
                pp2 = i % 2
                te.wait(sAC, tpp_free[pp2])
                t_tp = 0
                for c in range(DC):
                    t_tp = te.op(lambda t, par=par, pp2=pp2, c=c: t.transpose(
                        tpp[:, pp2, c, :], xl[:, par, c * 128:(c + 1) * 128],
                        ident_b[:, :]), inc=(sTE if c == DC - 1 else None))
                xl_free[par] = t_tp
                act.wait(sTE, t_tp)
                act.wait(sSP, win_free[w % 2])
                a_ev = act.op(lambda a, pp2=pp2, w=w, sub=sub: a.copy(
                    win[:, w % 2, :, sub * 128:(sub + 1) * 128], tpp[:, pp2, :, :]),
                    inc=sAC)
                tpp_free[pp2] = a_ev
                if sub == 3:
                    sp.wait(sAC, a_ev)
                    wr = sp.op(lambda s, w=w: s.dma_start(
                        xln1_r[:, :, w * 512:(w + 1) * 512], win[:, w % 2, :, :]),
                        inc=sSP, d=16)
                    wrote[w] = wr
                    win_free[w % 2] = wr

        barrier()
        if DBG:
            v = sp.op(lambda s: s.dma_start(d_xln1, xln1_hbm), inc=sSP, d=16)
            sp.wait(sSP, v)

        with ExitStack() as attn_ctx:
            attn_full = attn_ctx.enter_context(
                nc.sbuf_tensor("attn_full", [128, NP, T], BF16, side="right"))

            with ExitStack() as ph2:
                wq_pp = ph2.enter_context(nc.sbuf_tensor("wq_pp", [128, 2, DC, 384], BF16))
                xw = ph2.enter_context(nc.sbuf_tensor("xw", [128, 2, DC, 512], BF16))
                Kp = ph2.enter_context(nc.sbuf_tensor("Kp", [128, 2, NW, 512], BF16))
                Vp = ph2.enter_context(nc.sbuf_tensor("Vp", [128, 2, KC, 2, 128], BF16))
                Qp = ph2.enter_context(nc.sbuf_tensor("Qp", [128, 2, NG, 512], BF16))
                PT = ph2.enter_context(nc.sbuf_tensor("PT", [128, 4, 2, 512], BF16))
                sqKQ = ph2.enter_context(nc.sbuf_tensor("sqKQ", [128, 512], BF16))
                ot_f = ph2.enter_context(nc.sbuf_tensor("ot_f", [128, 2, 512], F32))
                lrl = ph2.enter_context(nc.sbuf_tensor("lrl", [128, 2], F32))
                onf = ph2.enter_context(nc.sbuf_tensor("onf", [128, 64], F32))
                negMall = ph2.enter_context(nc.sbuf_tensor("negMall", [128, 2, NG], F32))
                negM_b8 = ph2.enter_context(nc.sbuf_tensor("negM_b8", [1, 2, NG], BF16))
                kn2max = ph2.enter_context(nc.sbuf_tensor("kn2max", [1, 2, 2], F32))
                qn2g = ph2.enter_context(nc.sbuf_tensor("qn2g", [1, 2, 2 * NG], F32))
                msc = ph2.enter_context(nc.sbuf_tensor("msc", [1, 2, 2 * NG], F32))
                msc2 = ph2.enter_context(nc.sbuf_tensor("msc2", [1, 2, NG], F32))
                kn_t = ph2.enter_context(nc.sbuf_tensor("kn_t", [1, 2], F32))
                STS = ph2.enter_context(nc.psum_tensor("STS", [128, 4, 512], F32))
                OT = ph2.enter_context(nc.psum_tensor("OTp", [128, 2, 512], F32))
                PA = ph2.enter_context(nc.psum_tensor("PAp", [128, 512], F32))
                TPf = ph2.enter_context(nc.psum_tensor("TPf", [128, 384], F32))

                vz = dve.op(lambda v: v.memset(Vp[:, :, :, :, 64:], 0.0), inc=sDV)
                dve.wait(sDV, vz)
                v_vones = dve.op(lambda v: v.memset(Vp[:, :, :, :, 64:65], 1.0),
                                 inc=sDV)

                trk = dict(stp_free=[0, 0], pt_free=[0, 0, 0, 0], ot_free=[0, 0],
                           tpf_dv=0, tpf_ac=0, otf_rd=[0, 0], chain=None,
                           kq_rd=[0, 0], v_rd=[0, 0], xw_rd=[0, 0],
                           wq_rd=[0, 0], pa_free=[], xw_ld=[0] * NW)
                astate = [dict(kev=0, vev=0, qev=0, ndv=0) for _ in range(NP)]

                def emit_xw_load(w):
                    sp.wait(sTE, trk["xw_rd"][w % 2])
                    trk["xw_ld"][w] = sp.op(lambda s, w=w: s.dma_start(
                        xw[:, w % 2, :, :], xln1_r[:, :, w * 512:(w + 1) * 512]),
                        inc=sSP, d=16)

                def a_gen(p, ast):
                    par = p % 2
                    gp.wait(sTE, trk["wq_rd"][par])
                    for si, col0 in enumerate((p * 128, D + p * 128, 2 * D + p * 128)):
                        v_wq = gp.op(lambda g, par=par, si=si, col0=col0: g.dma_start(
                            wq_pp[:, par, :, si * 128:(si + 1) * 128],
                            wqkv_in[:, col0:col0 + 128]
                            .rearrange("(c p) n -> p c n", p=128)), inc=sGP, d=16)
                    emit_xw_load(0)
                    for w in range(NW):
                        wpar = w % 2
                        if w + 1 < NW:
                            emit_xw_load(w + 1)
                        te.wait(sSP, trk["xw_ld"][w])
                        te.wait(sGP, v_wq)
                        te.wait(sGP, v_brows)
                        te.wait(sGPc, v_consts)
                        te.wait(sDV, v_vones)
                        for s_, v_ in trk["pa_free"]:
                            te.wait(s_, v_)
                        for c in range(DC):
                            te.op(lambda t, c=c, wpar=wpar, par=par: t.matmul(
                                PA[:, :], wq_pp[:, par, c, 128:256],
                                xw[:, wpar, c, :], start=(c == 0), stop=False))
                        kmm = te.op(lambda t, p=p: t.matmul(
                            PA[:, :], bq_row[0:1, D + p * 128:D + p * 128 + 128],
                            ones_b[0:1, 0:512], start=False, stop=True), inc=sTE)
                        yield
                        act.wait(sTE, kmm)
                        act.wait(sTE, trk["kq_rd"][par])
                        a_kcp = act.op(lambda a, par=par, w=w: a.copy(
                            Kp[:, par, w, :], PA[:, :]), inc=sAC)
                        act.wait(sAC, a_kcp)
                        a_sqk = act.op(lambda a, par=par, w=w: a.activation(
                            sqKQ[:, :], Kp[:, par, w, :], AF.Square), inc=sAC)
                        ast["kev"] = a_kcp
                        trk["pa_free"] = [(sAC, a_sqk)]
                        last_mm = kmm
                        yield
                        te.wait(sAC, a_sqk)
                        for s_, v_ in trk["pa_free"]:
                            te.wait(s_, v_)
                        te.op(lambda t: t.matmul(
                            PA[0:1, :], ones_col[0:64, 0:1], sqKQ[0:64, :],
                            start=True, stop=True))
                        knm = te.op(lambda t: t.matmul(
                            PA[64:65, :], ones_col[64:128, 0:1], sqKQ[64:128, :],
                            start=True, stop=True), inc=sTE)
                        yield
                        dve.wait(sTE, knm)
                        if w == 0:
                            dve.op(lambda v, par=par: v.reduce_max(
                                kn2max[0:1, par, 0:1], PA[0:1, :], axis=AX.X))
                            knr = dve.op(lambda v, par=par: v.reduce_max(
                                kn2max[0:1, par, 1:2], PA[64:65, :], axis=AX.X),
                                inc=sDV)
                        else:
                            dve.op(lambda v: v.reduce_max(
                                kn_t[0:1, 0:1], PA[0:1, :], axis=AX.X))
                            drd = dve.op(lambda v: v.reduce_max(
                                kn_t[0:1, 1:2], PA[64:65, :], axis=AX.X), inc=sDV)
                            dve.wait(sDV, drd)
                            dve.op(lambda v, par=par: v.tensor_max(
                                kn2max[0:1, par, 0:1], kn2max[0:1, par, 0:1],
                                kn_t[0:1, 0:1]))
                            knr = dve.op(lambda v, par=par: v.tensor_max(
                                kn2max[0:1, par, 1:2], kn2max[0:1, par, 1:2],
                                kn_t[0:1, 1:2]), inc=sDV)
                        ast["ndv"] = knr
                        trk["pa_free"] = [(sDV, knr)]
                        yield
                        for s_, v_ in trk["pa_free"]:
                            te.wait(s_, v_)
                        for c in range(DC):
                            for t4 in range(4):
                                te.op(lambda t, c=c, wpar=wpar, t4=t4, par=par: t.matmul(
                                    PA[:, t4 * 128:(t4 + 1) * 128],
                                    xw[:, wpar, c, t4 * 128:(t4 + 1) * 128],
                                    wq_pp[:, par, c, 256:384],
                                    start=(c == 0 and t4 == 0), stop=False,
                                    skip_group_check=True))
                        vmm = 0
                        for t4 in range(4):
                            vmm = te.op(lambda t, p=p, t4=t4: t.matmul(
                                PA[:, t4 * 128:(t4 + 1) * 128], ones_b[0:1, 0:128],
                                bq_row[0:1, 2 * D + p * 128:2 * D + p * 128 + 128],
                                start=False, stop=(t4 == 3), skip_group_check=True),
                                inc=(sTE if t4 == 3 else None))
                        yield
                        dve.wait(sTE, vmm)
                        dve.wait(sTE, trk["v_rd"][par])
                        vev = dve.op(lambda v, par=par, w=w: v.tensor_copy(
                            Vp[:, par, w * 4:(w + 1) * 4, :, 0:64],
                            PA[:, 0:512].rearrange("p (t h d) -> p t h d", t=4, h=2)),
                            inc=sDV)
                        ast["vev"] = vev
                        trk["pa_free"] = [(sDV, vev)]
                        last_mm = vmm
                        yield
                        if w < NG:
                            for s_, v_ in trk["pa_free"]:
                                te.wait(s_, v_)
                            for c in range(DC):
                                te.op(lambda t, c=c, wpar=wpar, par=par: t.matmul(
                                    PA[:, :], wq_pp[:, par, c, 0:128],
                                    xw[:, wpar, c, :], start=(c == 0), stop=False))
                            qmm = te.op(lambda t, p=p: t.matmul(
                                PA[:, :], bq_row[0:1, p * 128:p * 128 + 128],
                                ones_b[0:1, 0:512], start=False, stop=True), inc=sTE)
                            yield
                            act.wait(sTE, qmm)
                            act.wait(sTE, trk["kq_rd"][par])
                            a_qcp = act.op(lambda a, par=par, w=w: a.activation(
                                Qp[:, par, w, :], PA[:, :], AF.Copy, scale=SCALE),
                                inc=sAC)
                            act.wait(sAC, a_qcp)
                            a_sqq = act.op(lambda a, par=par, w=w: a.activation(
                                sqKQ[:, :], Qp[:, par, w, :], AF.Square), inc=sAC)
                            ast["qev"] = a_qcp
                            trk["pa_free"] = [(sAC, a_sqq)]
                            last_mm = qmm
                            yield
                            te.wait(sAC, a_sqq)
                            for s_, v_ in trk["pa_free"]:
                                te.wait(s_, v_)
                            te.op(lambda t: t.matmul(
                                PA[0:1, :], ones_col[0:64, 0:1], sqKQ[0:64, :],
                                start=True, stop=True))
                            qnm = te.op(lambda t: t.matmul(
                                PA[64:65, :], ones_col[64:128, 0:1], sqKQ[64:128, :],
                                start=True, stop=True), inc=sTE)
                            yield
                            dve.wait(sTE, qnm)
                            dve.op(lambda v, par=par, w=w: v.reduce_max(
                                qn2g[0:1, par, w:w + 1], PA[0:1, :], axis=AX.X))
                            qnr = dve.op(lambda v, par=par, w=w: v.reduce_max(
                                qn2g[0:1, par, NG + w:NG + w + 1], PA[64:65, :],
                                axis=AX.X), inc=sDV)
                            ast["ndv"] = qnr
                            trk["pa_free"] = [(sDV, qnr)]
                            yield
                        trk["xw_rd"][wpar] = last_mm
                    trk["wq_rd"][par] = last_mm
                    yield
                    dve.wait(sDV, ast["ndv"])
                    dmu = 0
                    for gi in range(2 * NG):
                        h2, g2 = gi // NG, gi % NG
                        dmu = dve.op(lambda v, par=par, gi=gi, h2=h2, g2=g2: v.tensor_mul(
                            msc[0:1, par, gi:gi + 1],
                            qn2g[0:1, par, h2 * NG + g2:h2 * NG + g2 + 1],
                            kn2max[0:1, par, h2:h2 + 1]), inc=sDV)
                    dve.wait(sDV, dmu)
                    dmx = 0
                    for g2 in range(NG):
                        dmx = dve.op(lambda v, par=par, g2=g2: v.tensor_max(
                            msc2[0:1, par, g2:g2 + 1], msc[0:1, par, g2:g2 + 1],
                            msc[0:1, par, NG + g2:NG + g2 + 1]), inc=sDV)
                    act.wait(sDV, dmx)
                    aLn = act.op(lambda a, par=par: a.activation(
                        msc2[0:1, par, :], msc2[0:1, par, :], AF.Ln), inc=sAC)
                    act.wait(sAC, aLn)
                    aEx = act.op(lambda a, par=par: a.activation(
                        msc2[0:1, par, :], msc2[0:1, par, :], AF.Exp, scale=0.5),
                        inc=sAC)
                    dve.wait(sAC, aEx)
                    dnb = dve.op(lambda v, par=par: v.tensor_scalar(
                        negM_b8[0:1, par, :], msc2[0:1, par, :], -1.0, -0.05,
                        ALU.mult, ALU.add), inc=sDV)
                    yield
                    te.wait(sDV, dnb)
                    for s_, v_ in trk["pa_free"]:
                        te.wait(s_, v_)
                    bcm = 0
                    for g2 in range(NG):
                        bcm = te.op(lambda t, g2=g2, par=par: t.matmul(
                            PA[:, g2:g2 + 1], ones_b[0:1, 0:128],
                            negM_b8[0:1, par, g2:g2 + 1], start=True, stop=True),
                            inc=(sTE if g2 == NG - 1 else None))
                    yield
                    act.wait(sTE, bcm)
                    anm = act.op(lambda a, par=par: a.copy(
                        negMall[:, par, :], PA[:, 0:NG]), inc=sAC)
                    ast["anm"] = anm
                    trk["pa_free"] = [(sAC, anm)]

                def pump(g, n=1):
                    if g is None:
                        return
                    for _ in range(n):
                        try:
                            next(g)
                        except StopIteration:
                            return

                def drain(g):
                    if g is not None:
                        for _ in g:
                            pass

                gens = [a_gen(p, astate[p]) for p in range(NP)]
                drain(gens[0])

                for p in range(NP):
                    par = p % 2
                    nxt = gens[p + 1] if p + 1 < NP else None
                    ast = astate[p]
                    te.wait(sAC, ast["kev"])
                    te.wait(sAC, ast["qev"])
                    te.wait(sDV, ast["vev"])
                    te.wait(sDV, ast["ndv"])
                    anm = ast["anm"]

                    stm = 0
                    omm = 0

                    def norm_chain(p, g, par, omm_v):
                        oevs = [0, 0]
                        for h in range(2):
                            act.wait(sTE, max(omm_v, trk["otf_rd"][h]))
                            oev = act.op(lambda a, h=h: a.copy(
                                ot_f[0:65, h, :], OT[0:65, h, :]), inc=sAC)
                            trk["ot_free"][h] = oev
                            oevs[h] = oev
                            yield
                        for h in range(2):
                            hd = 64 * h
                            for qt in range(4):
                                tok = g * 512 + qt * 128
                                te.wait(sAC, oevs[h])
                                te.wait(sDV, trk["tpf_dv"])
                                te.wait(sAC, trk["tpf_ac"])
                                te.op(lambda t, qt=qt, h=h: t.transpose(
                                    TPf[:, 0:64],
                                    ot_f[0:64, h, qt * 128:(qt + 1) * 128],
                                    ident_f[0:64, 0:64]))
                                tq1 = te.op(lambda t, qt=qt, h=h: t.transpose(
                                    TPf[:, 64:65],
                                    ot_f[64:65, h, qt * 128:(qt + 1) * 128],
                                    ident_f[64:65, 64:65]), inc=sTE)
                                trk["otf_rd"][h] = tq1
                                dve.wait(sTE, tq1)
                                dcl = dve.op(lambda v: v.tensor_scalar_max(
                                    lrl[:, 0:1], TPf[:, 64:65], 1e-30), inc=sDV)
                                dve.wait(sDV, dcl)
                                drc = dve.op(lambda v: v.reciprocal(
                                    lrl[:, 1:2], lrl[:, 0:1]), inc=sDV)
                                dve.wait(sDV, drc)
                                ond = dve.op(lambda v: v.tensor_single_scalar(
                                    onf[:, 0:64], TPf[:, 0:64], lrl[:, 1:2],
                                    ALU.mult), inc=sDV)
                                trk["tpf_dv"] = ond
                                yield
                                te.wait(sDV, ond)
                                tb = te.op(lambda t: t.transpose(
                                    TPf[0:64, 128:256], onf[:, 0:64],
                                    ident_f[:, :]), inc=sTE)
                                act.wait(sTE, tb)
                                trk["tpf_ac"] = act.op(
                                    lambda a, hd=hd, p=p, tok=tok: a.copy(
                                        attn_full[hd:hd + 64, p, tok:tok + 128],
                                        TPf[0:64, 128:256]), inc=sAC)
                                yield

                    for g in range(NG):
                        stm_map = {}

                        def emit_st(sb, g=g, par=par):
                            pr = sb % 2
                            te.wait(sAC, trk["stp_free"][pr])
                            v = 0
                            for h in range(2):
                                hd = 64 * h
                                v = te.op(
                                    lambda t, sb=sb, pr=pr, h=h, hd=hd, g=g, par=par: t.matmul(
                                        STS[:, 2 * pr + h, :],
                                        Kp[hd:hd + 64, par, sb // 4,
                                           (sb % 4) * 128:(sb % 4) * 128 + 128],
                                        Qp[hd:hd + 64, par, g, :],
                                        start=True, stop=True),
                                    inc=(sTE if h == 1 else None))
                            stm_map[sb] = v
                            return v

                        ex_map = {}

                        def emit_o(sb, par=par):
                            nonlocal_omm = 0
                            slot = sb % 4
                            te.wait(sAC, ex_map[sb])
                            if sb == 0:
                                te.wait(sAC, trk["ot_free"][0])
                                te.wait(sAC, trk["ot_free"][1])
                            v = 0
                            for h in range(2):
                                v = te.op(
                                    lambda t, sb=sb, slot=slot, h=h, par=par: t.matmul(
                                        OT[:, h, :], Vp[:, par, sb, h, :],
                                        PT[:, slot, h, :],
                                        start=(sb == 0), stop=(sb == KC - 1),
                                        skip_group_check=True),
                                    inc=(sTE if h == 1 else None))
                            trk["pt_free"][slot] = v
                            return v

                        emit_st(0)
                        for sb in range(KC):
                            pr = sb % 2
                            slot = sb % 4
                            if sb + 1 < KC:
                                emit_st(sb + 1)
                            pump(trk["chain"], 1)
                            pump(nxt, 1)
                            stm = stm_map[sb]
                            act.wait(sTE, stm)
                            act.wait(sTE, trk["pt_free"][slot])
                            if sb == 0:
                                act.wait(sAC, anm)
                            ex = act.op(lambda a, pr=pr, slot=slot, g=g, par=par: a.activation(
                                PT[:, slot, :, :], STS[:, 2 * pr:2 * pr + 2, :],
                                AF.Exp, bias=negMall[:, par, g:g + 1]), inc=sAC)
                            trk["stp_free"][pr] = ex
                            ex_map[sb] = ex
                            if sb >= 1:
                                omm = emit_o(sb - 1)
                        omm = emit_o(KC - 1)
                        stm = stm_map[KC - 1]
                        drain(trk["chain"])
                        trk["chain"] = norm_chain(p, g, par, omm)
                        pump(trk["chain"], 2)
                    trk["kq_rd"][par] = stm
                    trk["v_rd"][par] = omm
                    drain(nxt)
                drain(trk["chain"])
                trk["chain"] = None
                barrier()

            if DBG:
                sp.wait(sAC, sAC.n)
                v = sp.op(lambda s: s.dma_start(
                    d_attn, attn_full.rearrange("p a t -> p (a t)")), inc=sSP, d=16)
                sp.wait(sSP, v)

            acc = root.enter_context(nc.sbuf_tensor("acc", [128, TT, D], F32))
            xln2 = root.enter_context(nc.sbuf_tensor("xln2", [128, DC, T], BF16))

            with ExitStack() as ph3:
                wproj_sb = ph3.enter_context(nc.sbuf_tensor("wproj_sb", [128, DC, D], BF16))
                xres = ph3.enter_context(nc.sbuf_tensor("xres", [128, 4, D], F32))
                xl2 = ph3.enter_context(nc.sbuf_tensor("xl2", [128, 2, D], BF16))
                sq2 = ph3.enter_context(nc.sbuf_tensor("sq2", [128, D], F32))
                st2 = ph3.enter_context(nc.sbuf_tensor("st2", [128, 4, 8], F32))
                PR = ph3.enter_context(nc.psum_tensor("PR", [128, 2, NH2, 512], F32))
                TP2 = ph3.enter_context(nc.psum_tensor("TP2", [128, DC, 128], BF16))

                v_wproj = gp.op(lambda g: g.dma_start(
                    wproj_sb[:, :, :], wproj_in.rearrange("(c p) n -> p c n", p=128)),
                    inc=sGP, d=16)

                xres_free = [0, 0, 0, 0]
                pr_free = [0, 0]
                xl2_free = [0, 0]
                tp2_free = 0
                ln2_prev = [None, None, None, None]
                for tt in range(TT):
                    par = tt % 4
                    par2 = tt % 2
                    sp.wait(sDV, xres_free[par])
                    xr = sp.op(lambda s, tt=tt, par=par: s.dma_start(
                        xres[:, par, :], x_in[tt * 128:(tt + 1) * 128, :]),
                        inc=sSP, d=16)
                    if tt == 0:
                        te.wait(sGP, v_wproj)
                    te.wait(sDV, pr_free[par2])
                    prmm = 0
                    for half in range(NH2):
                        for pp in range(NP):
                            te.op(lambda t, par2=par2, half=half, pp=pp, tt=tt: t.matmul(
                                PR[:, par2, half, :],
                                attn_full[:, pp, tt * 128:(tt + 1) * 128],
                                wproj_sb[:, pp, half * 512:(half + 1) * 512],
                                start=(pp == 0), stop=False))
                        prmm = te.op(lambda t, par2=par2, half=half: t.matmul(
                            PR[:, par2, half, :], ones_b[0:1, 0:128],
                            bp_row[0:1, half * 512:(half + 1) * 512],
                            start=False, stop=True), inc=sTE)
                    dve.wait(sTE, prmm)
                    dve.wait(sSP, xr)
                    d_add = 0
                    for half in range(NH2):
                        d_add = dve.op(lambda v, tt=tt, par2=par2, par=par, half=half: v.tensor_add(
                            acc[:, tt, half * 512:(half + 1) * 512],
                            PR[:, par2, half, :],
                            xres[:, par, half * 512:(half + 1) * 512]), inc=sDV)
                    xres_free[par] = d_add
                    pr_free[par2] = d_add
                    dve.wait(sTE, xl2_free[par2])
                    ln2_prev[par] = emit_ln(acc[:, tt, :], xl2[:, par2, :],
                                            st2[:, par, :], sq2[:, :],
                                            w_first=[(sDV, d_add)],
                                            prev=ln2_prev[par])
                    d_ln = ln2_prev[par]["d2"]
                    te.wait(sDV, d_ln)
                    te.wait(sAC, tp2_free)
                    ttp = 0
                    for c in range(DC):
                        ttp = te.op(lambda t, c=c, par2=par2: t.transpose(
                            TP2[:, c, :], xl2[:, par2, c * 128:(c + 1) * 128],
                            ident_b[:, :]),
                            inc=(sTE if c == DC - 1 else None))
                    xl2_free[par2] = ttp
                    act.wait(sTE, ttp)
                    tp2_free = act.op(lambda a, tt=tt: a.copy(
                        xln2[:, :, tt * 128:(tt + 1) * 128], TP2[:, :, :]), inc=sAC)

        barrier()
        if DBG:
            v = sp.op(lambda s: s.dma_start(
                d_x1.rearrange("(t p) d -> p t d", p=128), acc[:, :, :]),
                inc=sSP, d=16)
            v = sp.op(lambda s: s.dma_start(
                d_xln2, xln2.rearrange("p c t -> p (c t)")), inc=sSP, d=16)
            sp.wait(sSP, v)

        with ExitStack() as ph4:
            wgb = ph4.enter_context(nc.sbuf_tensor("wgb", [128, 2, DC, 512], BF16))
            wub = ph4.enter_context(nc.sbuf_tensor("wub", [128, 2, DC, 512], BF16))
            wdb = ph4.enter_context(nc.sbuf_tensor("wdb", [128, 8, D], BF16))
            h_q = ph4.enter_context(nc.sbuf_tensor("h_q", [128, 8, T], BF16))
            sg = ph4.enter_context(nc.sbuf_tensor("sg", [128, 2, 512], F32))
            GU = ph4.enter_context(nc.psum_tensor("GU", [128, 4, 512], F32))
            ACC = ph4.enter_context(nc.psum_tensor("ACCp", [128, 2, D], F32))

            stages = [(e, q) for e in range(E) for q in range(FQ)]
            wg_ld, wu_ld, wd_ld = {}, {}, {}
            wgb_free = [0, 0]
            wub_free = [0, 0]
            wdb_free = 0
            h_free = 0
            gu_free = [0, 0]
            sg_free = [0, 0]
            acc_ps_free = [0, 0]

            def emit_loads_gu(si):
                e, q = stages[si]
                for jh in range(2):
                    gp.wait(sTE, wgb_free[jh])
                    wg_ld[(si, jh)] = gp.op(lambda g, e=e, q=q, jh=jh: g.dma_start(
                        wgb[:, jh, :, :],
                        wg_in[e, :, q * 1024 + jh * 512:q * 1024 + (jh + 1) * 512]
                        .rearrange("(c p) f -> p c f", p=128)), inc=sGP, d=16)
                    gp.wait(sTE, wub_free[jh])
                    wu_ld[(si, jh)] = gp.op(lambda g, e=e, q=q, jh=jh: g.dma_start(
                        wub[:, jh, :, :],
                        wu_in[e, :, q * 1024 + jh * 512:q * 1024 + (jh + 1) * 512]
                        .rearrange("(c p) f -> p c f", p=128)), inc=sGP, d=16)

            def emit_load_wd(si):
                e, q = stages[si]
                gp.wait(sTE, wdb_free)
                wd_ld[si] = gp.op(lambda g, e=e, q=q: g.dma_start(
                    wdb[:, :, :],
                    wd_in[e, q * 1024:(q + 1) * 1024, :]
                    .rearrange("(c p) n -> p c n", p=128)), inc=sGP, d=16)

            emit_loads_gu(0)
            emit_load_wd(0)
            mult_val = {}
            for si in range(len(stages)):
                for fb in range(8):
                    jh = fb // 4
                    fo = (fb % 4) * 128
                    for tq in range(NG):
                        gb = 2 * (tq % 2)
                        ub = gb + 1
                        te.wait(sGP, max(wg_ld[(si, jh)], wu_ld[(si, jh)]))
                        te.wait(sDV, gu_free[tq % 2])
                        if fb == 0 and tq == 0:
                            te.wait(sTE, h_free)
                        for c in range(DC):
                            te.op(lambda t, c=c, jh=jh, fo=fo, gb=gb, tq=tq: t.matmul(
                                GU[:, gb, :], wgb[:, jh, c, fo:fo + 128],
                                xln2[:, c, tq * 512:(tq + 1) * 512],
                                start=(c == 0), stop=(c == DC - 1)))
                        gmm = 0
                        for c in range(DC):
                            gmm = te.op(lambda t, c=c, jh=jh, fo=fo, ub=ub, tq=tq: t.matmul(
                                GU[:, ub, :], wub[:, jh, c, fo:fo + 128],
                                xln2[:, c, tq * 512:(tq + 1) * 512],
                                start=(c == 0), stop=(c == DC - 1)),
                                inc=(sTE if c == DC - 1 else None))
                        if fb == 3 and tq == NG - 1:
                            wgb_free[0] = gmm
                            wub_free[0] = gmm
                        if fb == 7 and tq == NG - 1:
                            wgb_free[1] = gmm
                            wub_free[1] = gmm
                        act.wait(sTE, gmm)
                        act.wait(sDV, sg_free[tq % 2])
                        asl = act.op(lambda a, tq=tq, gb=gb: a.activation(
                            sg[:, tq % 2, :], GU[:, gb, :], AF.Silu), inc=sAC)
                        dve.wait(sAC, asl)
                        mlt = dve.op(lambda v, fb=fb, tq=tq, ub=ub: v.tensor_mul(
                            h_q[:, fb, tq * 512:(tq + 1) * 512], sg[:, tq % 2, :],
                            GU[:, ub, :]), inc=sDV)
                        gu_free[tq % 2] = mlt
                        sg_free[tq % 2] = mlt
                        mult_val[(fb, tq)] = mlt
                if si + 1 < len(stages):
                    emit_loads_gu(si + 1)
                bmm = 0
                for tt in range(TT):
                    apar = tt % 2
                    te.wait(sDV, mult_val[(7, tt // 4 if NG > 1 else 0)])
                    te.wait(sDV, acc_ps_free[apar])
                    if tt == 0:
                        te.wait(sGP, wd_ld[si])
                    for fc in range(8):
                        for half in range(NH2):
                            last = (fc == 7 and half == NH2 - 1)
                            v = te.op(lambda t, fc=fc, half=half, tt=tt, apar=apar: t.matmul(
                                ACC[:, apar, half * 512:(half + 1) * 512],
                                h_q[:, fc, tt * 128:(tt + 1) * 128],
                                wdb[:, fc, half * 512:(half + 1) * 512],
                                start=(fc == 0), stop=(fc == 7),
                                skip_group_check=True),
                                inc=(sTE if last else None))
                            if last:
                                bmm = v
                    dve.wait(sTE, bmm)
                    dadd = dve.op(lambda v, tt=tt, apar=apar: v.tensor_add(
                        acc[:, tt, :], acc[:, tt, :], ACC[:, apar, :]), inc=sDV)
                    acc_ps_free[apar] = dadd
                wdb_free = bmm
                h_free = bmm
                if si + 1 < len(stages):
                    emit_load_wd(si + 1)

            sp.wait(sDV, sDV.n)
            fin = sp.op(lambda s: s.dma_start(
                out_hbm.rearrange("(t p) d -> p t d", p=128), acc[:, :, :]),
                inc=sSP, d=16)
            sp.wait(sSP, fin)

        with nc.Block() as blk:
            def replay(name):
                items = qs[name].items

                def body(eng):
                    waited = {}
                    for it in items:
                        if it[0] == "w":
                            _, s, v = it
                            if waited.get(s.name, 0) < v:
                                eng.wait_ge(s.h, v)
                                waited[s.name] = v
                        else:
                            _, fn, inc, d = it
                            ins = fn(eng)
                            if inc is not None:
                                ins.then_inc(inc.h, d)
                return body

            blk.tensor(replay("te"))
            blk.scalar(replay("act"))
            blk.vector(replay("dve"))
            blk.gpsimd(replay("gp"))
            blk.sync(replay("sp"))

    n_items = {k: len(v.items) for k, v in qs.items()}
    print(f"[build] queue items: total={sum(n_items.values())} {n_items}")
    return nc


def kernel(L0_attn_inp, L0_attn_w_qkv, L0_attn_b_qkv, L0_attn_w_proj,
           L0_attn_b_proj, L0_ffn_Wg, L0_ffn_Wu, L0_ffn_Wd):
    cfg = full_cfg()
    B, S, D = cfg["B"], cfg["S"], cfg["D"]
    NC = cfg["NCORES"]
    CPB = NC // B
    T = S // CPB

    x = np.ascontiguousarray(np.asarray(L0_attn_inp, dtype=np.float32))
    wqkv = np.ascontiguousarray(np.asarray(L0_attn_w_qkv, dtype=np.float32))
    bqkv = np.ascontiguousarray(np.asarray(L0_attn_b_qkv, dtype=np.float32))
    wproj = np.ascontiguousarray(np.asarray(L0_attn_w_proj, dtype=np.float32))
    bproj = np.ascontiguousarray(np.asarray(L0_attn_b_proj, dtype=np.float32))
    wg = np.ascontiguousarray(np.asarray(L0_ffn_Wg, dtype=np.float32))
    wu = np.ascontiguousarray(np.asarray(L0_ffn_Wu, dtype=np.float32))
    wd = np.ascontiguousarray(np.asarray(L0_ffn_Wd, dtype=np.float32))

    nc = build(cfg)
    in_maps = []
    for c in range(NC):
        b = c // CPB
        qoff = (c % CPB) * T
        xb = np.ascontiguousarray(np.roll(x[b], -qoff, axis=0))
        in_maps.append(dict(x=xb, wqkv=wqkv, bqkv=bqkv, wproj=wproj,
                            bproj=bproj, wg=wg, wu=wu, wd=wd))
    res = run_bass_kernel_spmd(nc, in_maps, core_ids=list(range(NC)))
    out = np.empty((B, S, D), np.float32)
    for c, r in enumerate(res.results):
        b = c // CPB
        qoff = (c % CPB) * T
        out[b, qoff:qoff + T] = r["out"]
    return out


# version 10
# speedup vs baseline: 1.1545x; 1.1545x over previous
import numpy as np
from contextlib import ExitStack

import concourse.bass as bass
import concourse.mybir as mybir
from concourse.bass_utils import run_bass_kernel_spmd

F32 = mybir.dt.float32
BF16 = mybir.dt.bfloat16
AF = mybir.ActivationFunctionType
ALU = mybir.AluOpType
AX = mybir.AxisListType

ENGS = ("te", "act", "dve", "gp", "sp")


def full_cfg():
    return dict(B=2, S=8192, D=1024, H=16, FF=4096, E=8, NCORES=8)


class Sem:
    __slots__ = ("h", "n", "name")

    def __init__(self, h, name):
        self.h = h
        self.n = 0
        self.name = name

    def bump(self, d):
        self.n += d
        return self.n


class Q:
    def __init__(self):
        self.items = []

    def wait(self, sem, val):
        if val and val > 0:
            self.items.append(("w", sem, val))

    def op(self, fn, inc=None, d=1):
        tgt = inc.bump(d) if inc is not None else 0
        self.items.append(("o", fn, inc, d))
        return tgt


def build(cfg):
    B, S, D, H, FF, E, NCORES = (cfg[k] for k in ("B", "S", "D", "H", "FF", "E", "NCORES"))
    DH = 64
    SCALE = DH ** -0.5
    CPB = NCORES // B
    T = S // CPB
    DC = D // 128
    NP = H // 2
    NW = S // 512
    NT1 = S // 128
    TT = T // 128
    NG = T // 512
    KC = S // 128
    NSB = KC // 2
    FQ = FF // 1024
    NH2 = D // 512
    EPS = 1e-5

    nc = bass.Bass("TRN2", target_bir_lowering=False)

    x_in = nc.dram_tensor("x", [S, D], F32, kind="ExternalInput").ap()
    wqkv_in = nc.dram_tensor("wqkv", [D, 3 * D], F32, kind="ExternalInput").ap()
    bqkv_in = nc.dram_tensor("bqkv", [3 * D], F32, kind="ExternalInput").ap()
    wproj_in = nc.dram_tensor("wproj", [D, D], F32, kind="ExternalInput").ap()
    bproj_in = nc.dram_tensor("bproj", [D], F32, kind="ExternalInput").ap()
    wg_in = nc.dram_tensor("wg", [E, D, FF], F32, kind="ExternalInput").ap()
    wu_in = nc.dram_tensor("wu", [E, D, FF], F32, kind="ExternalInput").ap()
    wd_in = nc.dram_tensor("wd", [E, FF, D], F32, kind="ExternalInput").ap()
    out_hbm = nc.dram_tensor("out", [T, D], F32, kind="ExternalOutput").ap()
    xln1_hbm = nc.dram_tensor("xln1", [D, S], BF16, kind="Internal").ap()
    xln1_r = xln1_hbm.rearrange("(c p) t -> p c t", p=128)
    DBG = cfg.get("debug", False)
    if DBG:
        d_xln1 = nc.dram_tensor("d_xln1", [D, S], BF16, kind="ExternalOutput").ap()
        d_attn = nc.dram_tensor("d_attn", [128, NP * T], BF16, kind="ExternalOutput").ap()
        d_x1 = nc.dram_tensor("d_x1", [T, D], F32, kind="ExternalOutput").ap()
        d_xln2 = nc.dram_tensor("d_xln2", [128, DC * T], BF16, kind="ExternalOutput").ap()

    qs = {e: Q() for e in ENGS}
    te, act, dve, gp, sp = (qs[e] for e in ENGS)

    with ExitStack() as root:
        sems = {}
        for name in ("sSP", "sGP", "sGPc", "sTE", "sAC", "sDV"):
            sems[name] = Sem(root.enter_context(nc.semaphore(name)), name)
        sSP, sGP, sGPc, sTE, sAC, sDV = (sems[n] for n in
                                         ("sSP", "sGP", "sGPc", "sTE", "sAC", "sDV"))
        all_sems = list(sems.values())

        def barrier():
            snap = [(s, s.n) for s in all_sems]
            for e in ENGS:
                for s, v in snap:
                    qs[e].wait(s, v)

        ident_b = root.enter_context(nc.sbuf_tensor("ident_b", [128, 128], BF16))
        ident_f = root.enter_context(nc.sbuf_tensor("ident_f", [128, 128], F32))
        ones_b = root.enter_context(nc.sbuf_tensor("ones_b", [1, 512], BF16))
        ones_col = root.enter_context(nc.sbuf_tensor("ones_col", [128, 1], BF16))
        bq_row = root.enter_context(nc.sbuf_tensor("bq_row", [1, 3 * D], BF16))
        bp_row = root.enter_context(nc.sbuf_tensor("bp_row", [1, D], BF16))
        eps_col = root.enter_context(nc.sbuf_tensor("eps_col", [128, 1], F32))

        gp.op(lambda g: g.memset(ident_b[:, :], 0.0))
        gp.op(lambda g: g.affine_select(
            out=ident_b[:, :], in_=ident_b[:, :], compare_op=ALU.not_equal,
            fill=1.0, base=0, pattern=[[-1, 128]], channel_multiplier=1))
        gp.op(lambda g: g.memset(ident_f[:, :], 0.0))
        gp.op(lambda g: g.affine_select(
            out=ident_f[:, :], in_=ident_f[:, :], compare_op=ALU.not_equal,
            fill=1.0, base=0, pattern=[[-1, 128]], channel_multiplier=1))
        gp.op(lambda g: g.memset(ones_b[:, :], 1.0))
        gp.op(lambda g: g.memset(eps_col[:, :], EPS))
        v_consts = gp.op(lambda g: g.memset(ones_col[:, :], 1.0), inc=sGPc)
        gp.op(lambda g: g.dma_start(bq_row[0:1, :], bqkv_in[None, :]), inc=sGP, d=16)
        v_brows = gp.op(lambda g: g.dma_start(bp_row[0:1, :], bproj_in[None, :]),
                        inc=sGP, d=16)

        def emit_ln(x_ap, out_ap, stp, sq_ap, w_first=(), prev=None):
            ssum, negmean, msq, svar, slnv, srstd = (
                stp[:, i:i + 1] for i in range(6))
            sss = stp[:, 7:8]
            p = prev or dict(a1=0, a2=0, d1=0, d2=0, g0=0)
            for s, v in w_first:
                act.wait(s, v)
            act.wait(sDV, p["d2"])
            act.wait(sGPc, p["g0"])
            a0 = act.op(lambda a: a.activation(sq_ap, x_ap, AF.Square,
                                               accum_out=sss), inc=sAC)
            for s, v in w_first:
                dve.wait(s, v)
            dve.wait(sAC, p["a1"])
            d0 = dve.op(lambda v: v.reduce_sum(ssum, x_ap, axis=AX.X), inc=sDV)
            act.wait(sDV, d0)
            a1 = act.op(lambda a: a.activation(negmean, ssum, AF.Copy,
                                               scale=-1.0 / D), inc=sAC)
            gp.wait(sAC, a1)
            gp.wait(sDV, p["d1"])
            g0 = gp.op(lambda g: g.tensor_mul(msq, negmean, negmean), inc=sGPc)
            dve.wait(sAC, a0)
            dve.wait(sGPc, g0)
            d1 = dve.op(lambda v: v.scalar_tensor_tensor(
                svar, sss, 1.0 / D, msq, ALU.mult, ALU.subtract), inc=sDV)
            act.wait(sDV, d1)
            aL = act.op(lambda a: a.activation(slnv, svar, AF.Ln,
                                               bias=eps_col[:, 0:1]), inc=sAC)
            act.wait(sAC, aL)
            a2 = act.op(lambda a: a.activation(srstd, slnv, AF.Exp, scale=-0.5),
                        inc=sAC)
            dve.wait(sAC, a2)
            d2 = dve.op(lambda v: v.tensor_scalar(
                out_ap, x_ap, negmean, srstd, ALU.add, ALU.mult), inc=sDV)
            return dict(a1=a1, a2=a2, d1=d1, d2=d2, g0=g0)

        attn_s = ExitStack()
        ph2s = ExitStack()
        attn_full = attn_s.enter_context(
            nc.sbuf_tensor("attn_full", [128, NP, T], BF16, side="right"))
        wq_pp = ph2s.enter_context(nc.sbuf_tensor("wq_pp", [128, 2, DC, 384], BF16))
        xw = ph2s.enter_context(nc.sbuf_tensor("xw", [128, 2, DC, 512], BF16))
        Kp = ph2s.enter_context(nc.sbuf_tensor("Kp", [128, 2, NW, 512], BF16))
        Vp = ph2s.enter_context(nc.sbuf_tensor("Vp", [128, 2, KC, 2, 128], BF16))
        Qp = ph2s.enter_context(nc.sbuf_tensor("Qp", [128, 2, NG, 512], BF16))
        sqKQ = ph2s.enter_context(nc.sbuf_tensor("sqKQ", [128, 512], BF16))
        lrl = ph2s.enter_context(nc.sbuf_tensor("lrl", [128, 2], F32))
        onf = ph2s.enter_context(nc.sbuf_tensor("onf", [128, 64], F32))
        negMall = ph2s.enter_context(nc.sbuf_tensor("negMall", [128, 2, NG], F32))
        negM_b8 = ph2s.enter_context(nc.sbuf_tensor("negM_b8", [1, 2, NG], BF16))
        kn2max = ph2s.enter_context(nc.sbuf_tensor("kn2max", [1, 2, 2], F32))
        qn2g = ph2s.enter_context(nc.sbuf_tensor("qn2g", [1, 2, 2 * NG], F32))
        msc = ph2s.enter_context(nc.sbuf_tensor("msc", [1, 2, 2 * NG], F32))
        msc2 = ph2s.enter_context(nc.sbuf_tensor("msc2", [1, 2, NG], F32))
        kn_t = ph2s.enter_context(nc.sbuf_tensor("kn_t", [1, 2], F32))
        PA = ph2s.enter_context(nc.psum_tensor("PAp", [128, 512], F32))
        vz = dve.op(lambda v: v.memset(Vp[:, :, :, :, 64:], 0.0), inc=sDV)
        dve.wait(sDV, vz)
        wrote = [0] * NW
        v_vones = dve.op(lambda v: v.memset(Vp[:, :, :, :, 64:65], 1.0),
                         inc=sDV)

        trk = dict(stp_free=[0, 0], pt_free=[0, 0, 0, 0], ot_free=[0, 0],
                   tpf_dv=0, tpf_ac=0, otf_rd=[0, 0], chain=None,
                   kq_rd=[0, 0], v_rd=[0, 0], xw_rd=[0, 0],
                   wq_rd=[0, 0], pa_free=[], xw_ld=[0] * NW)
        astate = [dict(kev=0, vev=0, qev=0, ndv=0) for _ in range(NP)]

        def emit_xw_load(w):
            sp.wait(sTE, trk["xw_rd"][w % 2])
            sp.wait(sSP, wrote[w])
            trk["xw_ld"][w] = sp.op(lambda s, w=w: s.dma_start(
                xw[:, w % 2, :, :], xln1_r[:, :, w * 512:(w + 1) * 512]),
                inc=sSP, d=16)

        def a_gen(p, ast):
            par = p % 2
            gp.wait(sTE, trk["wq_rd"][par])
            for si, col0 in enumerate((p * 128, D + p * 128, 2 * D + p * 128)):
                v_wq = gp.op(lambda g, par=par, si=si, col0=col0: g.dma_start(
                    wq_pp[:, par, :, si * 128:(si + 1) * 128],
                    wqkv_in[:, col0:col0 + 128]
                    .rearrange("(c p) n -> p c n", p=128)), inc=sGP, d=16)
            emit_xw_load(0)
            for w in range(NW):
                wpar = w % 2
                if w + 1 < NW:
                    emit_xw_load(w + 1)
                te.wait(sSP, trk["xw_ld"][w])
                te.wait(sGP, v_wq)
                te.wait(sGP, v_brows)
                te.wait(sGPc, v_consts)
                te.wait(sDV, v_vones)
                for s_, v_ in trk["pa_free"]:
                    te.wait(s_, v_)
                for c in range(DC):
                    te.op(lambda t, c=c, wpar=wpar, par=par: t.matmul(
                        PA[:, :], wq_pp[:, par, c, 128:256],
                        xw[:, wpar, c, :], start=(c == 0), stop=False))
                kmm = te.op(lambda t, p=p: t.matmul(
                    PA[:, :], bq_row[0:1, D + p * 128:D + p * 128 + 128],
                    ones_b[0:1, 0:512], start=False, stop=True), inc=sTE)
                yield
                act.wait(sTE, kmm)
                act.wait(sTE, trk["kq_rd"][par])
                a_kcp = act.op(lambda a, par=par, w=w: a.copy(
                    Kp[:, par, w, :], PA[:, :]), inc=sAC)
                act.wait(sAC, a_kcp)
                a_sqk = act.op(lambda a, par=par, w=w: a.activation(
                    sqKQ[:, :], Kp[:, par, w, :], AF.Square), inc=sAC)
                ast["kev"] = a_kcp
                trk["pa_free"] = [(sAC, a_sqk)]
                last_mm = kmm
                yield
                te.wait(sAC, a_sqk)
                for s_, v_ in trk["pa_free"]:
                    te.wait(s_, v_)
                te.op(lambda t: t.matmul(
                    PA[0:1, :], ones_col[0:64, 0:1], sqKQ[0:64, :],
                    start=True, stop=True))
                knm = te.op(lambda t: t.matmul(
                    PA[64:65, :], ones_col[64:128, 0:1], sqKQ[64:128, :],
                    start=True, stop=True), inc=sTE)
                yield
                dve.wait(sTE, knm)
                if w == 0:
                    dve.op(lambda v, par=par: v.reduce_max(
                        kn2max[0:1, par, 0:1], PA[0:1, :], axis=AX.X))
                    knr = dve.op(lambda v, par=par: v.reduce_max(
                        kn2max[0:1, par, 1:2], PA[64:65, :], axis=AX.X),
                        inc=sDV)
                else:
                    dve.op(lambda v: v.reduce_max(
                        kn_t[0:1, 0:1], PA[0:1, :], axis=AX.X))
                    drd = dve.op(lambda v: v.reduce_max(
                        kn_t[0:1, 1:2], PA[64:65, :], axis=AX.X), inc=sDV)
                    dve.wait(sDV, drd)
                    dve.op(lambda v, par=par: v.tensor_max(
                        kn2max[0:1, par, 0:1], kn2max[0:1, par, 0:1],
                        kn_t[0:1, 0:1]))
                    knr = dve.op(lambda v, par=par: v.tensor_max(
                        kn2max[0:1, par, 1:2], kn2max[0:1, par, 1:2],
                        kn_t[0:1, 1:2]), inc=sDV)
                ast["ndv"] = knr
                trk["pa_free"] = [(sDV, knr)]
                yield
                for s_, v_ in trk["pa_free"]:
                    te.wait(s_, v_)
                for c in range(DC):
                    for t4 in range(4):
                        te.op(lambda t, c=c, wpar=wpar, t4=t4, par=par: t.matmul(
                            PA[:, t4 * 128:(t4 + 1) * 128],
                            xw[:, wpar, c, t4 * 128:(t4 + 1) * 128],
                            wq_pp[:, par, c, 256:384],
                            start=(c == 0 and t4 == 0), stop=False,
                            skip_group_check=True))
                vmm = 0
                for t4 in range(4):
                    vmm = te.op(lambda t, p=p, t4=t4: t.matmul(
                        PA[:, t4 * 128:(t4 + 1) * 128], ones_b[0:1, 0:128],
                        bq_row[0:1, 2 * D + p * 128:2 * D + p * 128 + 128],
                        start=False, stop=(t4 == 3), skip_group_check=True),
                        inc=(sTE if t4 == 3 else None))
                yield
                dve.wait(sTE, vmm)
                dve.wait(sTE, trk["v_rd"][par])
                vev = dve.op(lambda v, par=par, w=w: v.tensor_copy(
                    Vp[:, par, w * 4:(w + 1) * 4, :, 0:64],
                    PA[:, 0:512].rearrange("p (t h d) -> p t h d", t=4, h=2)),
                    inc=sDV)
                ast["vev"] = vev
                trk["pa_free"] = [(sDV, vev)]
                last_mm = vmm
                yield
                if w < NG:
                    for s_, v_ in trk["pa_free"]:
                        te.wait(s_, v_)
                    for c in range(DC):
                        te.op(lambda t, c=c, wpar=wpar, par=par: t.matmul(
                            PA[:, :], wq_pp[:, par, c, 0:128],
                            xw[:, wpar, c, :], start=(c == 0), stop=False))
                    qmm = te.op(lambda t, p=p: t.matmul(
                        PA[:, :], bq_row[0:1, p * 128:p * 128 + 128],
                        ones_b[0:1, 0:512], start=False, stop=True), inc=sTE)
                    yield
                    act.wait(sTE, qmm)
                    act.wait(sTE, trk["kq_rd"][par])
                    a_qcp = act.op(lambda a, par=par, w=w: a.activation(
                        Qp[:, par, w, :], PA[:, :], AF.Copy, scale=SCALE),
                        inc=sAC)
                    act.wait(sAC, a_qcp)
                    a_sqq = act.op(lambda a, par=par, w=w: a.activation(
                        sqKQ[:, :], Qp[:, par, w, :], AF.Square), inc=sAC)
                    ast["qev"] = a_qcp
                    trk["pa_free"] = [(sAC, a_sqq)]
                    last_mm = qmm
                    yield
                    te.wait(sAC, a_sqq)
                    for s_, v_ in trk["pa_free"]:
                        te.wait(s_, v_)
                    te.op(lambda t: t.matmul(
                        PA[0:1, :], ones_col[0:64, 0:1], sqKQ[0:64, :],
                        start=True, stop=True))
                    qnm = te.op(lambda t: t.matmul(
                        PA[64:65, :], ones_col[64:128, 0:1], sqKQ[64:128, :],
                        start=True, stop=True), inc=sTE)
                    yield
                    dve.wait(sTE, qnm)
                    dve.op(lambda v, par=par, w=w: v.reduce_max(
                        qn2g[0:1, par, w:w + 1], PA[0:1, :], axis=AX.X))
                    qnr = dve.op(lambda v, par=par, w=w: v.reduce_max(
                        qn2g[0:1, par, NG + w:NG + w + 1], PA[64:65, :],
                        axis=AX.X), inc=sDV)
                    ast["ndv"] = qnr
                    trk["pa_free"] = [(sDV, qnr)]
                    yield
                trk["xw_rd"][wpar] = last_mm
            trk["wq_rd"][par] = last_mm
            yield
            dve.wait(sDV, ast["ndv"])
            dmu = 0
            for gi in range(2 * NG):
                h2, g2 = gi // NG, gi % NG
                dmu = dve.op(lambda v, par=par, gi=gi, h2=h2, g2=g2: v.tensor_mul(
                    msc[0:1, par, gi:gi + 1],
                    qn2g[0:1, par, h2 * NG + g2:h2 * NG + g2 + 1],
                    kn2max[0:1, par, h2:h2 + 1]), inc=sDV)
            dve.wait(sDV, dmu)
            dmx = 0
            for g2 in range(NG):
                dmx = dve.op(lambda v, par=par, g2=g2: v.tensor_max(
                    msc2[0:1, par, g2:g2 + 1], msc[0:1, par, g2:g2 + 1],
                    msc[0:1, par, NG + g2:NG + g2 + 1]), inc=sDV)
            act.wait(sDV, dmx)
            aLn = act.op(lambda a, par=par: a.activation(
                msc2[0:1, par, :], msc2[0:1, par, :], AF.Ln), inc=sAC)
            act.wait(sAC, aLn)
            aEx = act.op(lambda a, par=par: a.activation(
                msc2[0:1, par, :], msc2[0:1, par, :], AF.Exp, scale=0.5),
                inc=sAC)
            dve.wait(sAC, aEx)
            dnb = dve.op(lambda v, par=par: v.tensor_scalar(
                negM_b8[0:1, par, :], msc2[0:1, par, :], -1.0, -0.05,
                ALU.mult, ALU.add), inc=sDV)
            yield
            te.wait(sDV, dnb)
            for s_, v_ in trk["pa_free"]:
                te.wait(s_, v_)
            bcm = 0
            for g2 in range(NG):
                bcm = te.op(lambda t, g2=g2, par=par: t.matmul(
                    PA[:, g2:g2 + 1], ones_b[0:1, 0:128],
                    negM_b8[0:1, par, g2:g2 + 1], start=True, stop=True),
                    inc=(sTE if g2 == NG - 1 else None))
            yield
            act.wait(sTE, bcm)
            anm = act.op(lambda a, par=par: a.copy(
                negMall[:, par, :], PA[:, 0:NG]), inc=sAC)
            ast["anm"] = anm
            trk["pa_free"] = [(sAC, anm)]

        def pump(g, n=1):
            if g is None:
                return
            for _ in range(n):
                try:
                    next(g)
                except StopIteration:
                    return

        def drain(g):
            if g is not None:
                for _ in g:
                    pass

        gens = [a_gen(p, astate[p]) for p in range(NP)]
        with ExitStack() as ph1:
            xt = ph1.enter_context(nc.sbuf_tensor("p1_xt", [128, 2, D], F32))
            sq = ph1.enter_context(nc.sbuf_tensor("p1_sq", [128, D], F32))
            xl = ph1.enter_context(nc.sbuf_tensor("p1_xl", [128, 2, D], BF16))
            win = ph1.enter_context(nc.sbuf_tensor("p1_win", [128, 2, DC, 512], BF16))
            st1 = ph1.enter_context(nc.sbuf_tensor("p1_st", [128, 2, 8], F32))
            tpp = ph1.enter_context(nc.psum_tensor("p1_tp", [128, 2, DC, 128], BF16))

            xt_free = [0, 0]
            xl_free = [0, 0]
            tpp_free = [0, 0]
            win_free = [0, 0]
            act.wait(sGPc, v_consts)
            dve.wait(sGPc, v_consts)
            te.wait(sGPc, v_consts)

            ln_prev = [None, None]
            for i in range(NT1):
                par = i % 2
                w, sub = i // 4, i % 4
                sp.wait(sDV, xt_free[par])
                ld = sp.op(lambda s, i=i, par=par: s.dma_start(
                    xt[:, par, :], x_in[i * 128:(i + 1) * 128, :]), inc=sSP, d=16)
                dve.wait(sTE, xl_free[par])
                ln_prev[par] = emit_ln(xt[:, par, :], xl[:, par, :],
                                       st1[:, par, :], sq[:, :],
                                       w_first=[(sSP, ld)], prev=ln_prev[par])
                d_ln = ln_prev[par]["d2"]
                xt_free[par] = d_ln
                te.wait(sDV, d_ln)
                pp2 = i % 2
                te.wait(sAC, tpp_free[pp2])
                t_tp = 0
                for c in range(DC):
                    t_tp = te.op(lambda t, par=par, pp2=pp2, c=c: t.transpose(
                        tpp[:, pp2, c, :], xl[:, par, c * 128:(c + 1) * 128],
                        ident_b[:, :]), inc=(sTE if c == DC - 1 else None))
                xl_free[par] = t_tp
                act.wait(sTE, t_tp)
                act.wait(sSP, win_free[w % 2])
                a_ev = act.op(lambda a, pp2=pp2, w=w, sub=sub: a.copy(
                    win[:, w % 2, :, sub * 128:(sub + 1) * 128], tpp[:, pp2, :, :]),
                    inc=sAC)
                tpp_free[pp2] = a_ev
                if sub == 3:
                    sp.wait(sAC, a_ev)
                    wr = sp.op(lambda s, w=w: s.dma_start(
                        xln1_r[:, :, w * 512:(w + 1) * 512], win[:, w % 2, :, :]),
                        inc=sSP, d=16)
                    wrote[w] = wr
                    win_free[w % 2] = wr
                    if w >= 1:
                        pump(gens[0], 10 if (w - 1) < NG else 6)

        drain(gens[0])
        barrier()
        if DBG:
            v = sp.op(lambda s: s.dma_start(d_xln1, xln1_hbm), inc=sSP, d=16)
            sp.wait(sSP, v)

        PT = ph2s.enter_context(nc.sbuf_tensor("PT", [128, 4, 2, 512], BF16))
        ot_f = ph2s.enter_context(nc.sbuf_tensor("ot_f", [128, 2, 512], F32))
        STS = ph2s.enter_context(nc.psum_tensor("STS", [128, 4, 512], F32))
        OT = ph2s.enter_context(nc.psum_tensor("OTp", [128, 2, 512], F32))
        TPf = ph2s.enter_context(nc.psum_tensor("TPf", [128, 384], F32))
        for p in range(NP):
            par = p % 2
            nxt = gens[p + 1] if p + 1 < NP else None
            ast = astate[p]
            te.wait(sAC, ast["kev"])
            te.wait(sAC, ast["qev"])
            te.wait(sDV, ast["vev"])
            te.wait(sDV, ast["ndv"])
            anm = ast["anm"]

            stm = 0
            omm = 0

            def norm_chain(p, g, par, omm_v):
                oevs = [0, 0]
                for h in range(2):
                    act.wait(sTE, max(omm_v, trk["otf_rd"][h]))
                    oev = act.op(lambda a, h=h: a.copy(
                        ot_f[0:65, h, :], OT[0:65, h, :]), inc=sAC)
                    trk["ot_free"][h] = oev
                    oevs[h] = oev
                    yield
                for h in range(2):
                    hd = 64 * h
                    for qt in range(4):
                        tok = g * 512 + qt * 128
                        te.wait(sAC, oevs[h])
                        te.wait(sDV, trk["tpf_dv"])
                        te.wait(sAC, trk["tpf_ac"])
                        te.op(lambda t, qt=qt, h=h: t.transpose(
                            TPf[:, 0:64],
                            ot_f[0:64, h, qt * 128:(qt + 1) * 128],
                            ident_f[0:64, 0:64]))
                        tq1 = te.op(lambda t, qt=qt, h=h: t.transpose(
                            TPf[:, 64:65],
                            ot_f[64:65, h, qt * 128:(qt + 1) * 128],
                            ident_f[64:65, 64:65]), inc=sTE)
                        trk["otf_rd"][h] = tq1
                        yield
                        dve.wait(sTE, tq1)
                        dcl = dve.op(lambda v: v.tensor_scalar_max(
                            lrl[:, 0:1], TPf[:, 64:65], 1e-30), inc=sDV)
                        dve.wait(sDV, dcl)
                        drc = dve.op(lambda v: v.reciprocal(
                            lrl[:, 1:2], lrl[:, 0:1]), inc=sDV)
                        dve.wait(sDV, drc)
                        ond = dve.op(lambda v: v.tensor_single_scalar(
                            onf[:, 0:64], TPf[:, 0:64], lrl[:, 1:2],
                            ALU.mult), inc=sDV)
                        trk["tpf_dv"] = ond
                        yield
                        te.wait(sDV, ond)
                        tb = te.op(lambda t: t.transpose(
                            TPf[0:64, 128:256], onf[:, 0:64],
                            ident_f[:, :]), inc=sTE)
                        yield
                        act.wait(sTE, tb)
                        trk["tpf_ac"] = act.op(
                            lambda a, hd=hd, p=p, tok=tok: a.copy(
                                attn_full[hd:hd + 64, p, tok:tok + 128],
                                TPf[0:64, 128:256]), inc=sAC)
                        yield

            for g in range(NG):
                stm_map = {}

                def emit_st(sb, g=g, par=par):
                    pr = sb % 2
                    te.wait(sAC, trk["stp_free"][pr])
                    v = 0
                    for h in range(2):
                        hd = 64 * h
                        v = te.op(
                            lambda t, sb=sb, pr=pr, h=h, hd=hd, g=g, par=par: t.matmul(
                                STS[:, 2 * pr + h, :],
                                Kp[hd:hd + 64, par, sb // 4,
                                   (sb % 4) * 128:(sb % 4) * 128 + 128],
                                Qp[hd:hd + 64, par, g, :],
                                start=True, stop=True),
                            inc=(sTE if h == 1 else None))
                    stm_map[sb] = v
                    return v

                ex_map = {}

                def emit_o(sb, par=par):
                    nonlocal_omm = 0
                    slot = sb % 4
                    te.wait(sAC, ex_map[sb])
                    if sb == 0:
                        te.wait(sAC, trk["ot_free"][0])
                        te.wait(sAC, trk["ot_free"][1])
                    v = 0
                    for h in range(2):
                        v = te.op(
                            lambda t, sb=sb, slot=slot, h=h, par=par: t.matmul(
                                OT[:, h, :], Vp[:, par, sb, h, :],
                                PT[:, slot, h, :],
                                start=(sb == 0), stop=(sb == KC - 1),
                                skip_group_check=True),
                            inc=(sTE if h == 1 else None))
                    trk["pt_free"][slot] = v
                    return v

                emit_st(0)
                for sb in range(KC):
                    pr = sb % 2
                    slot = sb % 4
                    if sb + 1 < KC:
                        emit_st(sb + 1)
                    pump(trk["chain"], 1)
                    pump(nxt, 1)
                    stm = stm_map[sb]
                    act.wait(sTE, stm)
                    act.wait(sTE, trk["pt_free"][slot])
                    if sb == 0:
                        act.wait(sAC, anm)
                    ex = act.op(lambda a, pr=pr, slot=slot, g=g, par=par: a.activation(
                        PT[:, slot, :, :], STS[:, 2 * pr:2 * pr + 2, :],
                        AF.Exp, bias=negMall[:, par, g:g + 1]), inc=sAC)
                    trk["stp_free"][pr] = ex
                    ex_map[sb] = ex
                    if sb >= 1:
                        omm = emit_o(sb - 1)
                omm = emit_o(KC - 1)
                stm = stm_map[KC - 1]
                drain(trk["chain"])
                trk["chain"] = norm_chain(p, g, par, omm)
                pump(trk["chain"], 2)
            trk["kq_rd"][par] = stm
            trk["v_rd"][par] = omm
            drain(nxt)
        drain(trk["chain"])
        trk["chain"] = None
        barrier()
        ph2s.close()

        if DBG:
            sp.wait(sAC, sAC.n)
            v = sp.op(lambda s: s.dma_start(
                d_attn, attn_full.rearrange("p a t -> p (a t)")), inc=sSP, d=16)
            sp.wait(sSP, v)

        acc = root.enter_context(nc.sbuf_tensor("acc", [128, TT, D], F32))
        xln2 = root.enter_context(nc.sbuf_tensor("xln2", [128, DC, T], BF16))

        with ExitStack() as ph3:
            wproj_sb = ph3.enter_context(nc.sbuf_tensor("wproj_sb", [128, DC, D], BF16))
            xres = ph3.enter_context(nc.sbuf_tensor("xres", [128, 4, D], F32))
            xl2 = ph3.enter_context(nc.sbuf_tensor("xl2", [128, 2, D], BF16))
            sq2 = ph3.enter_context(nc.sbuf_tensor("sq2", [128, D], F32))
            st2 = ph3.enter_context(nc.sbuf_tensor("st2", [128, 4, 8], F32))
            PR = ph3.enter_context(nc.psum_tensor("PR", [128, 2, NH2, 512], F32))
            TP2 = ph3.enter_context(nc.psum_tensor("TP2", [128, DC, 128], BF16))

            v_wproj = gp.op(lambda g: g.dma_start(
                wproj_sb[:, :, :], wproj_in.rearrange("(c p) n -> p c n", p=128)),
                inc=sGP, d=16)

            xres_free = [0, 0, 0, 0]
            pr_free = [0, 0]
            xl2_free = [0, 0]
            tp2_free = 0
            ln2_prev = [None, None, None, None]
            for tt in range(TT):
                par = tt % 4
                par2 = tt % 2
                sp.wait(sDV, xres_free[par])
                xr = sp.op(lambda s, tt=tt, par=par: s.dma_start(
                    xres[:, par, :], x_in[tt * 128:(tt + 1) * 128, :]),
                    inc=sSP, d=16)
                if tt == 0:
                    te.wait(sGP, v_wproj)
                te.wait(sDV, pr_free[par2])
                prmm = 0
                for half in range(NH2):
                    for pp in range(NP):
                        te.op(lambda t, par2=par2, half=half, pp=pp, tt=tt: t.matmul(
                            PR[:, par2, half, :],
                            attn_full[:, pp, tt * 128:(tt + 1) * 128],
                            wproj_sb[:, pp, half * 512:(half + 1) * 512],
                            start=(pp == 0), stop=False))
                    prmm = te.op(lambda t, par2=par2, half=half: t.matmul(
                        PR[:, par2, half, :], ones_b[0:1, 0:128],
                        bp_row[0:1, half * 512:(half + 1) * 512],
                        start=False, stop=True), inc=sTE)
                dve.wait(sTE, prmm)
                dve.wait(sSP, xr)
                d_add = 0
                for half in range(NH2):
                    d_add = dve.op(lambda v, tt=tt, par2=par2, par=par, half=half: v.tensor_add(
                        acc[:, tt, half * 512:(half + 1) * 512],
                        PR[:, par2, half, :],
                        xres[:, par, half * 512:(half + 1) * 512]), inc=sDV)
                xres_free[par] = d_add
                pr_free[par2] = d_add
                dve.wait(sTE, xl2_free[par2])
                ln2_prev[par] = emit_ln(acc[:, tt, :], xl2[:, par2, :],
                                        st2[:, par, :], sq2[:, :],
                                        w_first=[(sDV, d_add)],
                                        prev=ln2_prev[par])
                d_ln = ln2_prev[par]["d2"]
                te.wait(sDV, d_ln)
                te.wait(sAC, tp2_free)
                ttp = 0
                for c in range(DC):
                    ttp = te.op(lambda t, c=c, par2=par2: t.transpose(
                        TP2[:, c, :], xl2[:, par2, c * 128:(c + 1) * 128],
                        ident_b[:, :]),
                        inc=(sTE if c == DC - 1 else None))
                xl2_free[par2] = ttp
                act.wait(sTE, ttp)
                tp2_free = act.op(lambda a, tt=tt: a.copy(
                    xln2[:, :, tt * 128:(tt + 1) * 128], TP2[:, :, :]), inc=sAC)

        attn_s.close()
        barrier()
        if DBG:
            v = sp.op(lambda s: s.dma_start(
                d_x1.rearrange("(t p) d -> p t d", p=128), acc[:, :, :]),
                inc=sSP, d=16)
            v = sp.op(lambda s: s.dma_start(
                d_xln2, xln2.rearrange("p c t -> p (c t)")), inc=sSP, d=16)
            sp.wait(sSP, v)

        with ExitStack() as ph4:
            wgb = ph4.enter_context(nc.sbuf_tensor("wgb", [128, 2, DC, 512], BF16))
            wub = ph4.enter_context(nc.sbuf_tensor("wub", [128, 2, DC, 512], BF16))
            wdb = ph4.enter_context(nc.sbuf_tensor("wdb", [128, 8, D], BF16))
            h_q = ph4.enter_context(nc.sbuf_tensor("h_q", [128, 8, T], BF16))
            sg = ph4.enter_context(nc.sbuf_tensor("sg", [128, 2, 512], F32))
            GU = ph4.enter_context(nc.psum_tensor("GU", [128, 4, 512], F32))
            ACC = ph4.enter_context(nc.psum_tensor("ACCp", [128, 2, D], F32))

            stages = [(e, q) for e in range(E) for q in range(FQ)]
            wg_ld, wu_ld, wd_ld = {}, {}, {}
            wgb_free = [0, 0]
            wub_free = [0, 0]
            wdb_free = 0
            h_free = 0
            gu_free = [0, 0]
            sg_free = [0, 0]
            acc_ps_free = [0, 0]

            def emit_loads_gu(si):
                e, q = stages[si]
                for jh in range(2):
                    gp.wait(sTE, wgb_free[jh])
                    wg_ld[(si, jh)] = gp.op(lambda g, e=e, q=q, jh=jh: g.dma_start(
                        wgb[:, jh, :, :],
                        wg_in[e, :, q * 1024 + jh * 512:q * 1024 + (jh + 1) * 512]
                        .rearrange("(c p) f -> p c f", p=128)), inc=sGP, d=16)
                    gp.wait(sTE, wub_free[jh])
                    wu_ld[(si, jh)] = gp.op(lambda g, e=e, q=q, jh=jh: g.dma_start(
                        wub[:, jh, :, :],
                        wu_in[e, :, q * 1024 + jh * 512:q * 1024 + (jh + 1) * 512]
                        .rearrange("(c p) f -> p c f", p=128)), inc=sGP, d=16)

            def emit_load_wd(si):
                e, q = stages[si]
                gp.wait(sTE, wdb_free)
                wd_ld[si] = gp.op(lambda g, e=e, q=q: g.dma_start(
                    wdb[:, :, :],
                    wd_in[e, q * 1024:(q + 1) * 1024, :]
                    .rearrange("(c p) n -> p c n", p=128)), inc=sGP, d=16)

            emit_loads_gu(0)
            emit_load_wd(0)
            mult_val = {}
            for si in range(len(stages)):
                for fb in range(8):
                    jh = fb // 4
                    fo = (fb % 4) * 128
                    for tq in range(NG):
                        gb = 2 * (tq % 2)
                        ub = gb + 1
                        te.wait(sGP, max(wg_ld[(si, jh)], wu_ld[(si, jh)]))
                        te.wait(sDV, gu_free[tq % 2])
                        if fb == 0 and tq == 0:
                            te.wait(sTE, h_free)
                        for c in range(DC):
                            te.op(lambda t, c=c, jh=jh, fo=fo, gb=gb, tq=tq: t.matmul(
                                GU[:, gb, :], wgb[:, jh, c, fo:fo + 128],
                                xln2[:, c, tq * 512:(tq + 1) * 512],
                                start=(c == 0), stop=(c == DC - 1)))
                        gmm = 0
                        for c in range(DC):
                            gmm = te.op(lambda t, c=c, jh=jh, fo=fo, ub=ub, tq=tq: t.matmul(
                                GU[:, ub, :], wub[:, jh, c, fo:fo + 128],
                                xln2[:, c, tq * 512:(tq + 1) * 512],
                                start=(c == 0), stop=(c == DC - 1)),
                                inc=(sTE if c == DC - 1 else None))
                        if fb == 3 and tq == NG - 1:
                            wgb_free[0] = gmm
                            wub_free[0] = gmm
                        if fb == 7 and tq == NG - 1:
                            wgb_free[1] = gmm
                            wub_free[1] = gmm
                        act.wait(sTE, gmm)
                        act.wait(sDV, sg_free[tq % 2])
                        asl = act.op(lambda a, tq=tq, gb=gb: a.activation(
                            sg[:, tq % 2, :], GU[:, gb, :], AF.Silu), inc=sAC)
                        dve.wait(sAC, asl)
                        mlt = dve.op(lambda v, fb=fb, tq=tq, ub=ub: v.tensor_mul(
                            h_q[:, fb, tq * 512:(tq + 1) * 512], sg[:, tq % 2, :],
                            GU[:, ub, :]), inc=sDV)
                        gu_free[tq % 2] = mlt
                        sg_free[tq % 2] = mlt
                        mult_val[(fb, tq)] = mlt
                if si + 1 < len(stages):
                    emit_loads_gu(si + 1)
                bmm = 0
                for tt in range(TT):
                    apar = tt % 2
                    te.wait(sDV, mult_val[(7, tt // 4 if NG > 1 else 0)])
                    te.wait(sDV, acc_ps_free[apar])
                    if tt == 0:
                        te.wait(sGP, wd_ld[si])
                    for fc in range(8):
                        for half in range(NH2):
                            last = (fc == 7 and half == NH2 - 1)
                            v = te.op(lambda t, fc=fc, half=half, tt=tt, apar=apar: t.matmul(
                                ACC[:, apar, half * 512:(half + 1) * 512],
                                h_q[:, fc, tt * 128:(tt + 1) * 128],
                                wdb[:, fc, half * 512:(half + 1) * 512],
                                start=(fc == 0), stop=(fc == 7),
                                skip_group_check=True),
                                inc=(sTE if last else None))
                            if last:
                                bmm = v
                    dve.wait(sTE, bmm)
                    dadd = dve.op(lambda v, tt=tt, apar=apar: v.tensor_add(
                        acc[:, tt, :], acc[:, tt, :], ACC[:, apar, :]), inc=sDV)
                    acc_ps_free[apar] = dadd
                wdb_free = bmm
                h_free = bmm
                if si + 1 < len(stages):
                    emit_load_wd(si + 1)

            sp.wait(sDV, sDV.n)
            fin = sp.op(lambda s: s.dma_start(
                out_hbm.rearrange("(t p) d -> p t d", p=128), acc[:, :, :]),
                inc=sSP, d=16)
            sp.wait(sSP, fin)

        with nc.Block() as blk:
            def replay(name):
                items = qs[name].items

                def body(eng):
                    waited = {}
                    for it in items:
                        if it[0] == "w":
                            _, s, v = it
                            if waited.get(s.name, 0) < v:
                                eng.wait_ge(s.h, v)
                                waited[s.name] = v
                        else:
                            _, fn, inc, d = it
                            ins = fn(eng)
                            if inc is not None:
                                ins.then_inc(inc.h, d)
                return body

            blk.tensor(replay("te"))
            blk.scalar(replay("act"))
            blk.vector(replay("dve"))
            blk.gpsimd(replay("gp"))
            blk.sync(replay("sp"))

    n_items = {k: len(v.items) for k, v in qs.items()}
    print(f"[build] queue items: total={sum(n_items.values())} {n_items}")
    return nc


def kernel(L0_attn_inp, L0_attn_w_qkv, L0_attn_b_qkv, L0_attn_w_proj,
           L0_attn_b_proj, L0_ffn_Wg, L0_ffn_Wu, L0_ffn_Wd):
    cfg = full_cfg()
    B, S, D = cfg["B"], cfg["S"], cfg["D"]
    NC = cfg["NCORES"]
    CPB = NC // B
    T = S // CPB

    x = np.ascontiguousarray(np.asarray(L0_attn_inp, dtype=np.float32))
    wqkv = np.ascontiguousarray(np.asarray(L0_attn_w_qkv, dtype=np.float32))
    bqkv = np.ascontiguousarray(np.asarray(L0_attn_b_qkv, dtype=np.float32))
    wproj = np.ascontiguousarray(np.asarray(L0_attn_w_proj, dtype=np.float32))
    bproj = np.ascontiguousarray(np.asarray(L0_attn_b_proj, dtype=np.float32))
    wg = np.ascontiguousarray(np.asarray(L0_ffn_Wg, dtype=np.float32))
    wu = np.ascontiguousarray(np.asarray(L0_ffn_Wu, dtype=np.float32))
    wd = np.ascontiguousarray(np.asarray(L0_ffn_Wd, dtype=np.float32))

    nc = build(cfg)
    in_maps = []
    for c in range(NC):
        b = c // CPB
        qoff = (c % CPB) * T
        xb = np.ascontiguousarray(np.roll(x[b], -qoff, axis=0))
        in_maps.append(dict(x=xb, wqkv=wqkv, bqkv=bqkv, wproj=wproj,
                            bproj=bproj, wg=wg, wu=wu, wd=wd))
    res = run_bass_kernel_spmd(nc, in_maps, core_ids=list(range(NC)))
    out = np.empty((B, S, D), np.float32)
    for c, r in enumerate(res.results):
        b = c // CPB
        qoff = (c % CPB) * T
        out[b, qoff:qoff + T] = r["out"]
    return out


# version 11
# speedup vs baseline: 1.2996x; 1.1257x over previous
import numpy as np
from contextlib import ExitStack

import concourse.bass as bass
import concourse.mybir as mybir
from concourse.bass_utils import run_bass_kernel_spmd

F32 = mybir.dt.float32
BF16 = mybir.dt.bfloat16
AF = mybir.ActivationFunctionType
ALU = mybir.AluOpType
AX = mybir.AxisListType

ENGS = ("te", "act", "dve", "gp", "sp")


def full_cfg():
    return dict(B=2, S=8192, D=1024, H=16, FF=4096, E=8, NCORES=8)


class Sem:
    __slots__ = ("h", "n", "name")

    def __init__(self, h, name):
        self.h = h
        self.n = 0
        self.name = name

    def bump(self, d):
        self.n += d
        return self.n


class Q:
    def __init__(self):
        self.items = []

    def wait(self, sem, val):
        if val and val > 0:
            self.items.append(("w", sem, val))

    def op(self, fn, inc=None, d=1):
        tgt = inc.bump(d) if inc is not None else 0
        self.items.append(("o", fn, inc, d))
        return tgt


def build(cfg):
    B, S, D, H, FF, E, NCORES = (cfg[k] for k in ("B", "S", "D", "H", "FF", "E", "NCORES"))
    DH = 64
    SCALE = DH ** -0.5
    CPB = NCORES // B
    T = S // CPB
    DC = D // 128
    NP = H // 2
    NW = S // 512
    NT1 = S // 128
    TT = T // 128
    NG = T // 512
    KC = S // 128
    NSB = KC // 2
    FQ = FF // 1024
    NH2 = D // 512
    EPS = 1e-5

    nc = bass.Bass("TRN2", target_bir_lowering=False)

    x_in = nc.dram_tensor("x", [S, D], F32, kind="ExternalInput").ap()
    wqkv_in = nc.dram_tensor("wqkv", [D, 3 * D], F32, kind="ExternalInput").ap()
    bqkv_in = nc.dram_tensor("bqkv", [3 * D], F32, kind="ExternalInput").ap()
    wproj_in = nc.dram_tensor("wproj", [D, D], F32, kind="ExternalInput").ap()
    bproj_in = nc.dram_tensor("bproj", [D], F32, kind="ExternalInput").ap()
    wg_in = nc.dram_tensor("wg", [E, D, FF], F32, kind="ExternalInput").ap()
    wu_in = nc.dram_tensor("wu", [E, D, FF], F32, kind="ExternalInput").ap()
    wd_in = nc.dram_tensor("wd", [E, FF, D], F32, kind="ExternalInput").ap()
    out_hbm = nc.dram_tensor("out", [T, D], F32, kind="ExternalOutput").ap()
    xln1_hbm = nc.dram_tensor("xln1", [D, S], BF16, kind="Internal").ap()
    xln1_r = xln1_hbm.rearrange("(c p) t -> p c t", p=128)
    DBG = cfg.get("debug", False)
    if DBG:
        d_xln1 = nc.dram_tensor("d_xln1", [D, S], BF16, kind="ExternalOutput").ap()
        d_attn = nc.dram_tensor("d_attn", [128, NP * T], BF16, kind="ExternalOutput").ap()
        d_x1 = nc.dram_tensor("d_x1", [T, D], F32, kind="ExternalOutput").ap()
        d_xln2 = nc.dram_tensor("d_xln2", [128, DC * T], BF16, kind="ExternalOutput").ap()

    qs = {e: Q() for e in ENGS}
    te, act, dve, gp, sp = (qs[e] for e in ENGS)

    with ExitStack() as root:
        sems = {}
        for name in ("sSP", "sGP", "sGPc", "sTE", "sAC", "sDV"):
            sems[name] = Sem(root.enter_context(nc.semaphore(name)), name)
        sSP, sGP, sGPc, sTE, sAC, sDV = (sems[n] for n in
                                         ("sSP", "sGP", "sGPc", "sTE", "sAC", "sDV"))
        all_sems = list(sems.values())

        def barrier():
            snap = [(s, s.n) for s in all_sems]
            for e in ENGS:
                for s, v in snap:
                    qs[e].wait(s, v)

        ident_b = root.enter_context(nc.sbuf_tensor("ident_b", [128, 128], BF16))
        ident_f = root.enter_context(nc.sbuf_tensor("ident_f", [128, 128], F32))
        ones_b = root.enter_context(nc.sbuf_tensor("ones_b", [1, 512], BF16))
        ones_col = root.enter_context(nc.sbuf_tensor("ones_col", [128, 1], BF16))
        bq_row = root.enter_context(nc.sbuf_tensor("bq_row", [1, 3 * D], BF16))
        bp_row = root.enter_context(nc.sbuf_tensor("bp_row", [1, D], BF16))
        eps_col = root.enter_context(nc.sbuf_tensor("eps_col", [128, 1], F32))

        gp.op(lambda g: g.memset(ident_b[:, :], 0.0))
        gp.op(lambda g: g.affine_select(
            out=ident_b[:, :], in_=ident_b[:, :], compare_op=ALU.not_equal,
            fill=1.0, base=0, pattern=[[-1, 128]], channel_multiplier=1))
        gp.op(lambda g: g.memset(ident_f[:, :], 0.0))
        gp.op(lambda g: g.affine_select(
            out=ident_f[:, :], in_=ident_f[:, :], compare_op=ALU.not_equal,
            fill=1.0, base=0, pattern=[[-1, 128]], channel_multiplier=1))
        gp.op(lambda g: g.memset(ones_b[:, :], 1.0))
        gp.op(lambda g: g.memset(eps_col[:, :], EPS))
        v_consts = gp.op(lambda g: g.memset(ones_col[:, :], 1.0), inc=sGPc)
        gp.op(lambda g: g.dma_start(bq_row[0:1, :], bqkv_in[None, :]), inc=sGP, d=16)
        v_brows = gp.op(lambda g: g.dma_start(bp_row[0:1, :], bproj_in[None, :]),
                        inc=sGP, d=16)

        def emit_ln(x_ap, out_ap, stp, sq_ap, w_first=(), prev=None):
            ssum, negmean, msq, svar, slnv, srstd = (
                stp[:, i:i + 1] for i in range(6))
            sss = stp[:, 7:8]
            p = prev or dict(a1=0, a2=0, d1=0, d2=0, g0=0)
            for s, v in w_first:
                act.wait(s, v)
            act.wait(sDV, p["d2"])
            act.wait(sGPc, p["g0"])
            a0 = act.op(lambda a: a.activation(sq_ap, x_ap, AF.Square,
                                               accum_out=sss), inc=sAC)
            for s, v in w_first:
                dve.wait(s, v)
            dve.wait(sAC, p["a1"])
            d0 = dve.op(lambda v: v.reduce_sum(ssum, x_ap, axis=AX.X), inc=sDV)
            act.wait(sDV, d0)
            a1 = act.op(lambda a: a.activation(negmean, ssum, AF.Copy,
                                               scale=-1.0 / D), inc=sAC)
            gp.wait(sAC, a1)
            gp.wait(sDV, p["d1"])
            g0 = gp.op(lambda g: g.tensor_mul(msq, negmean, negmean), inc=sGPc)
            dve.wait(sAC, a0)
            dve.wait(sGPc, g0)
            d1 = dve.op(lambda v: v.scalar_tensor_tensor(
                svar, sss, 1.0 / D, msq, ALU.mult, ALU.subtract), inc=sDV)
            act.wait(sDV, d1)
            aL = act.op(lambda a: a.activation(slnv, svar, AF.Ln,
                                               bias=eps_col[:, 0:1]), inc=sAC)
            act.wait(sAC, aL)
            a2 = act.op(lambda a: a.activation(srstd, slnv, AF.Exp, scale=-0.5),
                        inc=sAC)
            dve.wait(sAC, a2)
            d2 = dve.op(lambda v: v.tensor_scalar(
                out_ap, x_ap, negmean, srstd, ALU.add, ALU.mult), inc=sDV)
            return dict(a1=a1, a2=a2, d1=d1, d2=d2, g0=g0)

        attn_s = ExitStack()
        ph2s = ExitStack()
        attn_full = attn_s.enter_context(
            nc.sbuf_tensor("attn_full", [128, NP, T], BF16, side="right"))
        wq_pp = ph2s.enter_context(nc.sbuf_tensor("wq_pp", [128, 2, DC, 384], BF16))
        xw = ph2s.enter_context(nc.sbuf_tensor("xw", [128, 2, DC, 512], BF16))
        Kp = ph2s.enter_context(nc.sbuf_tensor("Kp", [128, 2, NW, 512], BF16))
        Vp = ph2s.enter_context(nc.sbuf_tensor("Vp", [128, 2, KC, 2, 128], BF16))
        Qp = ph2s.enter_context(nc.sbuf_tensor("Qp", [128, 2, NG, 512], BF16))
        sqKQ = ph2s.enter_context(nc.sbuf_tensor("sqKQ", [128, 512], BF16))
        lrl = ph2s.enter_context(nc.sbuf_tensor("lrl", [128, 2], F32))
        onf = ph2s.enter_context(nc.sbuf_tensor("onf", [128, 64], F32))
        negMall = ph2s.enter_context(nc.sbuf_tensor("negMall", [128, 2, NG], F32))
        negM_b8 = ph2s.enter_context(nc.sbuf_tensor("negM_b8", [1, 2, NG], BF16))
        kn2max = ph2s.enter_context(nc.sbuf_tensor("kn2max", [1, 2, 2], F32))
        qn2g = ph2s.enter_context(nc.sbuf_tensor("qn2g", [1, 2, 2 * NG], F32))
        msc = ph2s.enter_context(nc.sbuf_tensor("msc", [1, 2, 2 * NG], F32))
        msc2 = ph2s.enter_context(nc.sbuf_tensor("msc2", [1, 2, NG], F32))
        kn_t = ph2s.enter_context(nc.sbuf_tensor("kn_t", [1, 2], F32))
        PA = ph2s.enter_context(nc.psum_tensor("PAp", [128, 512], F32))
        vz = dve.op(lambda v: v.memset(Vp[:, :, :, :, 64:], 0.0), inc=sDV)
        dve.wait(sDV, vz)
        wrote = [0] * NW
        v_vones = dve.op(lambda v: v.memset(Vp[:, :, :, :, 64:65], 1.0),
                         inc=sDV)

        trk = dict(stp_free=[0, 0, 0, 0], pt_free=[0, 0, 0, 0], ot_free=[0, 0],
                   tpf_dv=0, tpf_ac=0, otf_rd=[0, 0], chain=None,
                   kq_rd=[0, 0], v_rd=[0, 0], xw_rd=[0, 0],
                   wq_rd=[0, 0], pa_free=[], xw_ld=[0] * NW)
        astate = [dict(kev=0, vev=0, qev=0, ndv=0) for _ in range(NP)]

        def emit_xw_load(w):
            sp.wait(sTE, trk["xw_rd"][w % 2])
            sp.wait(sSP, wrote[w])
            trk["xw_ld"][w] = sp.op(lambda s, w=w: s.dma_start(
                xw[:, w % 2, :, :], xln1_r[:, :, w * 512:(w + 1) * 512]),
                inc=sSP, d=16)

        def a_gen(p, ast):
            par = p % 2
            gp.wait(sTE, trk["wq_rd"][par])
            for si, col0 in enumerate((p * 128, D + p * 128, 2 * D + p * 128)):
                v_wq = gp.op(lambda g, par=par, si=si, col0=col0: g.dma_start(
                    wq_pp[:, par, :, si * 128:(si + 1) * 128],
                    wqkv_in[:, col0:col0 + 128]
                    .rearrange("(c p) n -> p c n", p=128)), inc=sGP, d=16)
            emit_xw_load(0)
            for w in range(NW):
                wpar = w % 2
                if w + 1 < NW:
                    emit_xw_load(w + 1)
                te.wait(sSP, trk["xw_ld"][w])
                te.wait(sGP, v_wq)
                te.wait(sGP, v_brows)
                te.wait(sGPc, v_consts)
                te.wait(sDV, v_vones)
                for s_, v_ in trk["pa_free"]:
                    te.wait(s_, v_)
                for c in range(DC):
                    te.op(lambda t, c=c, wpar=wpar, par=par: t.matmul(
                        PA[:, :], wq_pp[:, par, c, 128:256],
                        xw[:, wpar, c, :], start=(c == 0), stop=False))
                kmm = te.op(lambda t, p=p: t.matmul(
                    PA[:, :], bq_row[0:1, D + p * 128:D + p * 128 + 128],
                    ones_b[0:1, 0:512], start=False, stop=True), inc=sTE)
                yield
                dve.wait(sTE, kmm)
                dve.wait(sTE, trk["kq_rd"][par])
                a_kcp = dve.op(lambda v, par=par, w=w: v.tensor_copy(
                    Kp[:, par, w, :], PA[:, :]), inc=sDV)
                dve.wait(sDV, a_kcp)
                a_sqk = dve.op(lambda v, par=par, w=w: v.tensor_mul(
                    sqKQ[:, :], Kp[:, par, w, :], Kp[:, par, w, :]), inc=sDV)
                ast["kev"] = a_sqk
                trk["pa_free"] = [(sDV, a_sqk)]
                last_mm = kmm
                yield
                te.wait(sDV, a_sqk)
                for s_, v_ in trk["pa_free"]:
                    te.wait(s_, v_)
                te.op(lambda t: t.matmul(
                    PA[0:1, :], ones_col[0:64, 0:1], sqKQ[0:64, :],
                    start=True, stop=True))
                knm = te.op(lambda t: t.matmul(
                    PA[64:65, :], ones_col[64:128, 0:1], sqKQ[64:128, :],
                    start=True, stop=True), inc=sTE)
                yield
                dve.wait(sTE, knm)
                if w == 0:
                    dve.op(lambda v, par=par: v.reduce_max(
                        kn2max[0:1, par, 0:1], PA[0:1, :], axis=AX.X))
                    knr = dve.op(lambda v, par=par: v.reduce_max(
                        kn2max[0:1, par, 1:2], PA[64:65, :], axis=AX.X),
                        inc=sDV)
                else:
                    dve.op(lambda v: v.reduce_max(
                        kn_t[0:1, 0:1], PA[0:1, :], axis=AX.X))
                    drd = dve.op(lambda v: v.reduce_max(
                        kn_t[0:1, 1:2], PA[64:65, :], axis=AX.X), inc=sDV)
                    dve.wait(sDV, drd)
                    dve.op(lambda v, par=par: v.tensor_max(
                        kn2max[0:1, par, 0:1], kn2max[0:1, par, 0:1],
                        kn_t[0:1, 0:1]))
                    knr = dve.op(lambda v, par=par: v.tensor_max(
                        kn2max[0:1, par, 1:2], kn2max[0:1, par, 1:2],
                        kn_t[0:1, 1:2]), inc=sDV)
                ast["ndv"] = knr
                trk["pa_free"] = [(sDV, knr)]
                yield
                for s_, v_ in trk["pa_free"]:
                    te.wait(s_, v_)
                for c in range(DC):
                    for t4 in range(4):
                        te.op(lambda t, c=c, wpar=wpar, t4=t4, par=par: t.matmul(
                            PA[:, t4 * 128:(t4 + 1) * 128],
                            xw[:, wpar, c, t4 * 128:(t4 + 1) * 128],
                            wq_pp[:, par, c, 256:384],
                            start=(c == 0 and t4 == 0), stop=False,
                            skip_group_check=True))
                vmm = 0
                for t4 in range(4):
                    vmm = te.op(lambda t, p=p, t4=t4: t.matmul(
                        PA[:, t4 * 128:(t4 + 1) * 128], ones_b[0:1, 0:128],
                        bq_row[0:1, 2 * D + p * 128:2 * D + p * 128 + 128],
                        start=False, stop=(t4 == 3), skip_group_check=True),
                        inc=(sTE if t4 == 3 else None))
                yield
                dve.wait(sTE, vmm)
                dve.wait(sTE, trk["v_rd"][par])
                vev = dve.op(lambda v, par=par, w=w: v.tensor_copy(
                    Vp[:, par, w * 4:(w + 1) * 4, :, 0:64],
                    PA[:, 0:512].rearrange("p (t h d) -> p t h d", t=4, h=2)),
                    inc=sDV)
                ast["vev"] = vev
                trk["pa_free"] = [(sDV, vev)]
                last_mm = vmm
                yield
                if w < NG:
                    for s_, v_ in trk["pa_free"]:
                        te.wait(s_, v_)
                    for c in range(DC):
                        te.op(lambda t, c=c, wpar=wpar, par=par: t.matmul(
                            PA[:, :], wq_pp[:, par, c, 0:128],
                            xw[:, wpar, c, :], start=(c == 0), stop=False))
                    qmm = te.op(lambda t, p=p: t.matmul(
                        PA[:, :], bq_row[0:1, p * 128:p * 128 + 128],
                        ones_b[0:1, 0:512], start=False, stop=True), inc=sTE)
                    yield
                    dve.wait(sTE, qmm)
                    dve.wait(sTE, trk["kq_rd"][par])
                    a_qcp = dve.op(lambda v, par=par, w=w: v.tensor_single_scalar(
                        Qp[:, par, w, :], PA[:, :], SCALE, ALU.mult), inc=sDV)
                    dve.wait(sDV, a_qcp)
                    a_sqq = dve.op(lambda v, par=par, w=w: v.tensor_mul(
                        sqKQ[:, :], Qp[:, par, w, :], Qp[:, par, w, :]), inc=sDV)
                    ast["qev"] = a_sqq
                    trk["pa_free"] = [(sDV, a_sqq)]
                    last_mm = qmm
                    yield
                    te.wait(sDV, a_sqq)
                    for s_, v_ in trk["pa_free"]:
                        te.wait(s_, v_)
                    te.op(lambda t: t.matmul(
                        PA[0:1, :], ones_col[0:64, 0:1], sqKQ[0:64, :],
                        start=True, stop=True))
                    qnm = te.op(lambda t: t.matmul(
                        PA[64:65, :], ones_col[64:128, 0:1], sqKQ[64:128, :],
                        start=True, stop=True), inc=sTE)
                    yield
                    dve.wait(sTE, qnm)
                    dve.op(lambda v, par=par, w=w: v.reduce_max(
                        qn2g[0:1, par, w:w + 1], PA[0:1, :], axis=AX.X))
                    qnr = dve.op(lambda v, par=par, w=w: v.reduce_max(
                        qn2g[0:1, par, NG + w:NG + w + 1], PA[64:65, :],
                        axis=AX.X), inc=sDV)
                    ast["ndv"] = qnr
                    trk["pa_free"] = [(sDV, qnr)]
                    yield
                trk["xw_rd"][wpar] = last_mm
            trk["wq_rd"][par] = last_mm
            yield
            dve.wait(sDV, ast["ndv"])
            dmu = 0
            for gi in range(2 * NG):
                h2, g2 = gi // NG, gi % NG
                dmu = dve.op(lambda v, par=par, gi=gi, h2=h2, g2=g2: v.tensor_mul(
                    msc[0:1, par, gi:gi + 1],
                    qn2g[0:1, par, h2 * NG + g2:h2 * NG + g2 + 1],
                    kn2max[0:1, par, h2:h2 + 1]), inc=sDV)
            dve.wait(sDV, dmu)
            dmx = 0
            for g2 in range(NG):
                dmx = dve.op(lambda v, par=par, g2=g2: v.tensor_max(
                    msc2[0:1, par, g2:g2 + 1], msc[0:1, par, g2:g2 + 1],
                    msc[0:1, par, NG + g2:NG + g2 + 1]), inc=sDV)
            act.wait(sDV, dmx)
            aLn = act.op(lambda a, par=par: a.activation(
                msc2[0:1, par, :], msc2[0:1, par, :], AF.Ln), inc=sAC)
            act.wait(sAC, aLn)
            aEx = act.op(lambda a, par=par: a.activation(
                msc2[0:1, par, :], msc2[0:1, par, :], AF.Exp, scale=0.5),
                inc=sAC)
            dve.wait(sAC, aEx)
            dnb = dve.op(lambda v, par=par: v.tensor_scalar(
                negM_b8[0:1, par, :], msc2[0:1, par, :], -1.0, -0.05,
                ALU.mult, ALU.add), inc=sDV)
            yield
            te.wait(sDV, dnb)
            te.wait(sDV, trk["tpf_ac"])
            for s_, v_ in trk["pa_free"]:
                te.wait(s_, v_)
            bcm = 0
            for g2 in range(NG):
                bcm = te.op(lambda t, g2=g2, par=par: t.matmul(
                    PA[:, g2:g2 + 1], ones_b[0:1, 0:128],
                    negM_b8[0:1, par, g2:g2 + 1], start=True, stop=True),
                    inc=(sTE if g2 == NG - 1 else None))
            yield
            dve.wait(sTE, bcm)
            anm = dve.op(lambda v, par=par: v.tensor_copy(
                negMall[:, par, :], PA[:, 0:NG]), inc=sDV)
            ast["anm"] = anm
            trk["pa_free"] = [(sDV, anm)]

        def pump(g, n=1):
            if g is None:
                return
            for _ in range(n):
                try:
                    next(g)
                except StopIteration:
                    return

        def drain(g):
            if g is not None:
                for _ in g:
                    pass

        gens = [a_gen(p, astate[p]) for p in range(NP)]
        with ExitStack() as ph1:
            xt = ph1.enter_context(nc.sbuf_tensor("p1_xt", [128, 2, D], F32))
            sq = ph1.enter_context(nc.sbuf_tensor("p1_sq", [128, D], F32))
            xl = ph1.enter_context(nc.sbuf_tensor("p1_xl", [128, 2, D], BF16))
            win = ph1.enter_context(nc.sbuf_tensor("p1_win", [128, 2, DC, 512], BF16))
            st1 = ph1.enter_context(nc.sbuf_tensor("p1_st", [128, 2, 8], F32))
            tpp = ph1.enter_context(nc.psum_tensor("p1_tp", [128, 2, DC, 128], BF16))

            xt_free = [0, 0]
            xl_free = [0, 0]
            tpp_free = [0, 0]
            win_free = [0, 0]
            act.wait(sGPc, v_consts)
            dve.wait(sGPc, v_consts)
            te.wait(sGPc, v_consts)

            ln_prev = [None, None]
            for i in range(NT1):
                par = i % 2
                w, sub = i // 4, i % 4
                sp.wait(sDV, xt_free[par])
                ld = sp.op(lambda s, i=i, par=par: s.dma_start(
                    xt[:, par, :], x_in[i * 128:(i + 1) * 128, :]), inc=sSP, d=16)
                dve.wait(sTE, xl_free[par])
                ln_prev[par] = emit_ln(xt[:, par, :], xl[:, par, :],
                                       st1[:, par, :], sq[:, :],
                                       w_first=[(sSP, ld)], prev=ln_prev[par])
                d_ln = ln_prev[par]["d2"]
                xt_free[par] = d_ln
                te.wait(sDV, d_ln)
                pp2 = i % 2
                te.wait(sAC, tpp_free[pp2])
                t_tp = 0
                for c in range(DC):
                    t_tp = te.op(lambda t, par=par, pp2=pp2, c=c: t.transpose(
                        tpp[:, pp2, c, :], xl[:, par, c * 128:(c + 1) * 128],
                        ident_b[:, :]), inc=(sTE if c == DC - 1 else None))
                xl_free[par] = t_tp
                act.wait(sTE, t_tp)
                act.wait(sSP, win_free[w % 2])
                a_ev = act.op(lambda a, pp2=pp2, w=w, sub=sub: a.copy(
                    win[:, w % 2, :, sub * 128:(sub + 1) * 128], tpp[:, pp2, :, :]),
                    inc=sAC)
                tpp_free[pp2] = a_ev
                if sub == 3:
                    sp.wait(sAC, a_ev)
                    wr = sp.op(lambda s, w=w: s.dma_start(
                        xln1_r[:, :, w * 512:(w + 1) * 512], win[:, w % 2, :, :]),
                        inc=sSP, d=16)
                    wrote[w] = wr
                    win_free[w % 2] = wr
                    if w >= 1:
                        pump(gens[0], 10 if (w - 1) < NG else 6)

        drain(gens[0])
        barrier()
        if DBG:
            v = sp.op(lambda s: s.dma_start(d_xln1, xln1_hbm), inc=sSP, d=16)
            sp.wait(sSP, v)

        PT = ph2s.enter_context(nc.sbuf_tensor("PT", [128, 4, 2, 512], BF16))
        ot_f = ph2s.enter_context(nc.sbuf_tensor("ot_f", [128, 2, 512], F32))
        STS = ph2s.enter_context(nc.psum_tensor("STS", [128, 4, 512], F32))
        OT = ph2s.enter_context(nc.psum_tensor("OTp", [128, 2, 512], F32))
        TPf = ph2s.enter_context(nc.psum_tensor("TPf", [128, 384], F32))
        for p in range(NP):
            par = p % 2
            nxt = gens[p + 1] if p + 1 < NP else None
            ast = astate[p]
            te.wait(sDV, ast["kev"])
            te.wait(sDV, ast["qev"])
            te.wait(sDV, ast["vev"])
            te.wait(sDV, ast["ndv"])
            anm = ast["anm"]

            stm = 0
            omm = 0

            def norm_chain(p, g, par, omm_v):
                oevs = [0, 0]
                for h in range(2):
                    dve.wait(sTE, max(omm_v, trk["otf_rd"][h]))
                    oev = dve.op(lambda v, h=h: v.tensor_copy(
                        ot_f[0:65, h, :], OT[0:65, h, :]), inc=sDV)
                    trk["ot_free"][h] = oev
                    oevs[h] = oev
                    yield
                for h in range(2):
                    hd = 64 * h
                    for qt in range(4):
                        tok = g * 512 + qt * 128
                        te.wait(sDV, oevs[h])
                        te.wait(sDV, trk["tpf_dv"])
                        te.wait(sDV, trk["tpf_ac"])
                        te.op(lambda t, qt=qt, h=h: t.transpose(
                            TPf[:, 0:64],
                            ot_f[0:64, h, qt * 128:(qt + 1) * 128],
                            ident_f[0:64, 0:64]))
                        tq1 = te.op(lambda t, qt=qt, h=h: t.transpose(
                            TPf[:, 64:65],
                            ot_f[64:65, h, qt * 128:(qt + 1) * 128],
                            ident_f[64:65, 64:65]), inc=sTE)
                        trk["otf_rd"][h] = tq1
                        yield
                        dve.wait(sTE, tq1)
                        dcl = dve.op(lambda v: v.tensor_scalar_max(
                            lrl[:, 0:1], TPf[:, 64:65], 1e-30), inc=sDV)
                        dve.wait(sDV, dcl)
                        drc = dve.op(lambda v: v.reciprocal(
                            lrl[:, 1:2], lrl[:, 0:1]), inc=sDV)
                        dve.wait(sDV, drc)
                        ond = dve.op(lambda v: v.tensor_single_scalar(
                            onf[:, 0:64], TPf[:, 0:64], lrl[:, 1:2],
                            ALU.mult), inc=sDV)
                        trk["tpf_dv"] = ond
                        yield
                        te.wait(sDV, ond)
                        tb = te.op(lambda t: t.transpose(
                            TPf[0:64, 128:256], onf[:, 0:64],
                            ident_f[:, :]), inc=sTE)
                        yield
                        dve.wait(sTE, tb)
                        trk["tpf_ac"] = dve.op(
                            lambda v, hd=hd, p=p, tok=tok: v.tensor_copy(
                                attn_full[hd:hd + 64, p, tok:tok + 128],
                                TPf[0:64, 128:256]), inc=sDV)
                        yield

            for g in range(NG):
                stm_map = {}

                def emit_st(sb, g=g, par=par):
                    pr = sb % 2
                    te.wait(sAC, trk["stp_free"][pr])
                    v = 0
                    for h in range(2):
                        hd = 64 * h
                        v = te.op(
                            lambda t, sb=sb, pr=pr, h=h, hd=hd, g=g, par=par: t.matmul(
                                STS[:, 2 * pr + h, :],
                                Kp[hd:hd + 64, par, sb // 4,
                                   (sb % 4) * 128:(sb % 4) * 128 + 128],
                                Qp[hd:hd + 64, par, g, :],
                                start=True, stop=True),
                            inc=(sTE if h == 1 else None))
                    stm_map[sb] = v
                    return v

                ex_map = {}

                def emit_o(sb, par=par):
                    nonlocal_omm = 0
                    slot = sb % 4
                    te.wait(sAC, ex_map[sb])
                    if sb == 0:
                        te.wait(sDV, trk["ot_free"][0])
                        te.wait(sDV, trk["ot_free"][1])
                    v = 0
                    for h in range(2):
                        v = te.op(
                            lambda t, sb=sb, slot=slot, h=h, par=par: t.matmul(
                                OT[:, h, :], Vp[:, par, sb, h, :],
                                PT[:, slot, h, :],
                                start=(sb == 0), stop=(sb == KC - 1),
                                skip_group_check=True),
                            inc=(sTE if h == 1 else None))
                    trk["pt_free"][slot] = v
                    return v

                emit_st(0)
                for sb in range(KC):
                    pr = sb % 2
                    slot = sb % 4
                    if sb + 1 < KC:
                        emit_st(sb + 1)
                    pump(trk["chain"], 1)
                    pump(nxt, 1)
                    stm = stm_map[sb]
                    act.wait(sTE, stm)
                    act.wait(sTE, trk["pt_free"][slot])
                    if sb == 0:
                        act.wait(sDV, anm)
                    ex = act.op(lambda a, pr=pr, slot=slot, g=g, par=par: a.activation(
                        PT[:, slot, :, :], STS[:, 2 * pr:2 * pr + 2, :],
                        AF.Exp, bias=negMall[:, par, g:g + 1]), inc=sAC)
                    trk["stp_free"][pr] = ex
                    ex_map[sb] = ex
                    if sb >= 1:
                        omm = emit_o(sb - 1)
                omm = emit_o(KC - 1)
                stm = stm_map[KC - 1]
                drain(trk["chain"])
                trk["chain"] = norm_chain(p, g, par, omm)
                pump(trk["chain"], 2)
            trk["kq_rd"][par] = stm
            trk["v_rd"][par] = omm
            drain(nxt)
        drain(trk["chain"])
        trk["chain"] = None
        barrier()
        ph2s.close()

        if DBG:
            sp.wait(sAC, sAC.n)
            v = sp.op(lambda s: s.dma_start(
                d_attn, attn_full.rearrange("p a t -> p (a t)")), inc=sSP, d=16)
            sp.wait(sSP, v)

        acc = root.enter_context(nc.sbuf_tensor("acc", [128, TT, D], F32))
        xln2 = root.enter_context(nc.sbuf_tensor("xln2", [128, DC, T], BF16))

        with ExitStack() as ph3:
            wproj_sb = ph3.enter_context(nc.sbuf_tensor("wproj_sb", [128, DC, D], BF16))
            xres = ph3.enter_context(nc.sbuf_tensor("xres", [128, 4, D], F32))
            xl2 = ph3.enter_context(nc.sbuf_tensor("xl2", [128, 2, D], BF16))
            sq2 = ph3.enter_context(nc.sbuf_tensor("sq2", [128, D], F32))
            st2 = ph3.enter_context(nc.sbuf_tensor("st2", [128, 4, 8], F32))
            PR = ph3.enter_context(nc.psum_tensor("PR", [128, 2, NH2, 512], F32))
            TP2 = ph3.enter_context(nc.psum_tensor("TP2", [128, DC, 128], BF16))

            v_wproj = gp.op(lambda g: g.dma_start(
                wproj_sb[:, :, :], wproj_in.rearrange("(c p) n -> p c n", p=128)),
                inc=sGP, d=16)

            xres_free = [0, 0, 0, 0]
            pr_free = [0, 0]
            xl2_free = [0, 0]
            tp2_free = 0
            ln2_prev = [None, None, None, None]
            for tt in range(TT):
                par = tt % 4
                par2 = tt % 2
                sp.wait(sDV, xres_free[par])
                xr = sp.op(lambda s, tt=tt, par=par: s.dma_start(
                    xres[:, par, :], x_in[tt * 128:(tt + 1) * 128, :]),
                    inc=sSP, d=16)
                if tt == 0:
                    te.wait(sGP, v_wproj)
                te.wait(sDV, pr_free[par2])
                prmm = 0
                for half in range(NH2):
                    for pp in range(NP):
                        te.op(lambda t, par2=par2, half=half, pp=pp, tt=tt: t.matmul(
                            PR[:, par2, half, :],
                            attn_full[:, pp, tt * 128:(tt + 1) * 128],
                            wproj_sb[:, pp, half * 512:(half + 1) * 512],
                            start=(pp == 0), stop=False))
                    prmm = te.op(lambda t, par2=par2, half=half: t.matmul(
                        PR[:, par2, half, :], ones_b[0:1, 0:128],
                        bp_row[0:1, half * 512:(half + 1) * 512],
                        start=False, stop=True), inc=sTE)
                dve.wait(sTE, prmm)
                dve.wait(sSP, xr)
                d_add = 0
                for half in range(NH2):
                    d_add = dve.op(lambda v, tt=tt, par2=par2, par=par, half=half: v.tensor_add(
                        acc[:, tt, half * 512:(half + 1) * 512],
                        PR[:, par2, half, :],
                        xres[:, par, half * 512:(half + 1) * 512]), inc=sDV)
                xres_free[par] = d_add
                pr_free[par2] = d_add
                dve.wait(sTE, xl2_free[par2])
                ln2_prev[par] = emit_ln(acc[:, tt, :], xl2[:, par2, :],
                                        st2[:, par, :], sq2[:, :],
                                        w_first=[(sDV, d_add)],
                                        prev=ln2_prev[par])
                d_ln = ln2_prev[par]["d2"]
                te.wait(sDV, d_ln)
                te.wait(sAC, tp2_free)
                ttp = 0
                for c in range(DC):
                    ttp = te.op(lambda t, c=c, par2=par2: t.transpose(
                        TP2[:, c, :], xl2[:, par2, c * 128:(c + 1) * 128],
                        ident_b[:, :]),
                        inc=(sTE if c == DC - 1 else None))
                xl2_free[par2] = ttp
                act.wait(sTE, ttp)
                tp2_free = act.op(lambda a, tt=tt: a.copy(
                    xln2[:, :, tt * 128:(tt + 1) * 128], TP2[:, :, :]), inc=sAC)

        attn_s.close()
        barrier()
        if DBG:
            v = sp.op(lambda s: s.dma_start(
                d_x1.rearrange("(t p) d -> p t d", p=128), acc[:, :, :]),
                inc=sSP, d=16)
            v = sp.op(lambda s: s.dma_start(
                d_xln2, xln2.rearrange("p c t -> p (c t)")), inc=sSP, d=16)
            sp.wait(sSP, v)

        with ExitStack() as ph4:
            wgb = ph4.enter_context(nc.sbuf_tensor("wgb", [128, 2, DC, 512], BF16))
            wub = ph4.enter_context(nc.sbuf_tensor("wub", [128, 2, DC, 512], BF16))
            wdb = ph4.enter_context(nc.sbuf_tensor("wdb", [128, 8, D], BF16))
            h_q = ph4.enter_context(nc.sbuf_tensor("h_q", [128, 8, T], BF16))
            sg = ph4.enter_context(nc.sbuf_tensor("sg", [128, 2, 512], F32))
            GU = ph4.enter_context(nc.psum_tensor("GU", [128, 4, 512], F32))
            ACC = ph4.enter_context(nc.psum_tensor("ACCp", [128, 2, D], F32))

            stages = [(e, q) for e in range(E) for q in range(FQ)]
            wg_ld, wu_ld, wd_ld = {}, {}, {}
            wgb_free = [0, 0]
            wub_free = [0, 0]
            wdb_free = 0
            h_free = 0
            gu_free = [0, 0]
            sg_free = [0, 0]
            acc_ps_free = [0, 0]

            def emit_loads_gu(si):
                e, q = stages[si]
                for jh in range(2):
                    gp.wait(sTE, wgb_free[jh])
                    wg_ld[(si, jh)] = gp.op(lambda g, e=e, q=q, jh=jh: g.dma_start(
                        wgb[:, jh, :, :],
                        wg_in[e, :, q * 1024 + jh * 512:q * 1024 + (jh + 1) * 512]
                        .rearrange("(c p) f -> p c f", p=128)), inc=sGP, d=16)
                    gp.wait(sTE, wub_free[jh])
                    wu_ld[(si, jh)] = gp.op(lambda g, e=e, q=q, jh=jh: g.dma_start(
                        wub[:, jh, :, :],
                        wu_in[e, :, q * 1024 + jh * 512:q * 1024 + (jh + 1) * 512]
                        .rearrange("(c p) f -> p c f", p=128)), inc=sGP, d=16)

            def emit_load_wd(si):
                e, q = stages[si]
                gp.wait(sTE, wdb_free)
                wd_ld[si] = gp.op(lambda g, e=e, q=q: g.dma_start(
                    wdb[:, :, :],
                    wd_in[e, q * 1024:(q + 1) * 1024, :]
                    .rearrange("(c p) n -> p c n", p=128)), inc=sGP, d=16)

            emit_loads_gu(0)
            emit_load_wd(0)
            mult_val = {}
            for si in range(len(stages)):
                for fb in range(8):
                    jh = fb // 4
                    fo = (fb % 4) * 128
                    for tq in range(NG):
                        gb = 2 * (tq % 2)
                        ub = gb + 1
                        te.wait(sGP, max(wg_ld[(si, jh)], wu_ld[(si, jh)]))
                        te.wait(sDV, gu_free[tq % 2])
                        if fb == 0 and tq == 0:
                            te.wait(sTE, h_free)
                        for c in range(DC):
                            te.op(lambda t, c=c, jh=jh, fo=fo, gb=gb, tq=tq: t.matmul(
                                GU[:, gb, :], wgb[:, jh, c, fo:fo + 128],
                                xln2[:, c, tq * 512:(tq + 1) * 512],
                                start=(c == 0), stop=(c == DC - 1)))
                        gmm = 0
                        for c in range(DC):
                            gmm = te.op(lambda t, c=c, jh=jh, fo=fo, ub=ub, tq=tq: t.matmul(
                                GU[:, ub, :], wub[:, jh, c, fo:fo + 128],
                                xln2[:, c, tq * 512:(tq + 1) * 512],
                                start=(c == 0), stop=(c == DC - 1)),
                                inc=(sTE if c == DC - 1 else None))
                        if fb == 3 and tq == NG - 1:
                            wgb_free[0] = gmm
                            wub_free[0] = gmm
                        if fb == 7 and tq == NG - 1:
                            wgb_free[1] = gmm
                            wub_free[1] = gmm
                        act.wait(sTE, gmm)
                        act.wait(sDV, sg_free[tq % 2])
                        asl = act.op(lambda a, tq=tq, gb=gb: a.activation(
                            sg[:, tq % 2, :], GU[:, gb, :], AF.Silu), inc=sAC)
                        dve.wait(sAC, asl)
                        mlt = dve.op(lambda v, fb=fb, tq=tq, ub=ub: v.tensor_mul(
                            h_q[:, fb, tq * 512:(tq + 1) * 512], sg[:, tq % 2, :],
                            GU[:, ub, :]), inc=sDV)
                        gu_free[tq % 2] = mlt
                        sg_free[tq % 2] = mlt
                        mult_val[(fb, tq)] = mlt
                if si + 1 < len(stages):
                    emit_loads_gu(si + 1)
                bmm = 0
                for tt in range(TT):
                    apar = tt % 2
                    te.wait(sDV, mult_val[(7, tt // 4 if NG > 1 else 0)])
                    te.wait(sDV, acc_ps_free[apar])
                    if tt == 0:
                        te.wait(sGP, wd_ld[si])
                    for fc in range(8):
                        for half in range(NH2):
                            last = (fc == 7 and half == NH2 - 1)
                            v = te.op(lambda t, fc=fc, half=half, tt=tt, apar=apar: t.matmul(
                                ACC[:, apar, half * 512:(half + 1) * 512],
                                h_q[:, fc, tt * 128:(tt + 1) * 128],
                                wdb[:, fc, half * 512:(half + 1) * 512],
                                start=(fc == 0), stop=(fc == 7),
                                skip_group_check=True),
                                inc=(sTE if last else None))
                            if last:
                                bmm = v
                    dve.wait(sTE, bmm)
                    dadd = dve.op(lambda v, tt=tt, apar=apar: v.tensor_add(
                        acc[:, tt, :], acc[:, tt, :], ACC[:, apar, :]), inc=sDV)
                    acc_ps_free[apar] = dadd
                wdb_free = bmm
                h_free = bmm
                if si + 1 < len(stages):
                    emit_load_wd(si + 1)

            sp.wait(sDV, sDV.n)
            fin = sp.op(lambda s: s.dma_start(
                out_hbm.rearrange("(t p) d -> p t d", p=128), acc[:, :, :]),
                inc=sSP, d=16)
            sp.wait(sSP, fin)

        with nc.Block() as blk:
            def replay(name):
                items = qs[name].items

                def body(eng):
                    waited = {}
                    for it in items:
                        if it[0] == "w":
                            _, s, v = it
                            if waited.get(s.name, 0) < v:
                                eng.wait_ge(s.h, v)
                                waited[s.name] = v
                        else:
                            _, fn, inc, d = it
                            ins = fn(eng)
                            if inc is not None:
                                ins.then_inc(inc.h, d)
                return body

            blk.tensor(replay("te"))
            blk.scalar(replay("act"))
            blk.vector(replay("dve"))
            blk.gpsimd(replay("gp"))
            blk.sync(replay("sp"))

    n_items = {k: len(v.items) for k, v in qs.items()}
    print(f"[build] queue items: total={sum(n_items.values())} {n_items}")
    return nc


def kernel(L0_attn_inp, L0_attn_w_qkv, L0_attn_b_qkv, L0_attn_w_proj,
           L0_attn_b_proj, L0_ffn_Wg, L0_ffn_Wu, L0_ffn_Wd):
    cfg = full_cfg()
    B, S, D = cfg["B"], cfg["S"], cfg["D"]
    NC = cfg["NCORES"]
    CPB = NC // B
    T = S // CPB

    x = np.ascontiguousarray(np.asarray(L0_attn_inp, dtype=np.float32))
    wqkv = np.ascontiguousarray(np.asarray(L0_attn_w_qkv, dtype=np.float32))
    bqkv = np.ascontiguousarray(np.asarray(L0_attn_b_qkv, dtype=np.float32))
    wproj = np.ascontiguousarray(np.asarray(L0_attn_w_proj, dtype=np.float32))
    bproj = np.ascontiguousarray(np.asarray(L0_attn_b_proj, dtype=np.float32))
    wg = np.ascontiguousarray(np.asarray(L0_ffn_Wg, dtype=np.float32))
    wu = np.ascontiguousarray(np.asarray(L0_ffn_Wu, dtype=np.float32))
    wd = np.ascontiguousarray(np.asarray(L0_ffn_Wd, dtype=np.float32))

    nc = build(cfg)
    in_maps = []
    for c in range(NC):
        b = c // CPB
        qoff = (c % CPB) * T
        xb = np.ascontiguousarray(np.roll(x[b], -qoff, axis=0))
        in_maps.append(dict(x=xb, wqkv=wqkv, bqkv=bqkv, wproj=wproj,
                            bproj=bproj, wg=wg, wu=wu, wd=wd))
    res = run_bass_kernel_spmd(nc, in_maps, core_ids=list(range(NC)))
    out = np.empty((B, S, D), np.float32)
    for c, r in enumerate(res.results):
        b = c // CPB
        qoff = (c % CPB) * T
        out[b, qoff:qoff + T] = r["out"]
    return out
